# version 01
# baseline (speedup 1.0000x reference)
import contextlib
import numpy as np
import ml_dtypes
import concourse.bass as bass
import concourse.mybir as mybir
from concourse.bass_utils import run_bass_kernel_spmd

F32 = mybir.dt.float32
BF16 = mybir.dt.bfloat16
AF = mybir.ActivationFunctionType
ALU = mybir.AluOpType
AX = mybir.AxisListType
BF = ml_dtypes.bfloat16

NC = 8
D = 2048
CD = D // 128
T = 1024
F1 = 5632
NH, HD = 16, 128
FBIG = 3.0e38
KILL = -1.0e12
LN_EPS = 1e-5
LN_EPS_IN = 1e-12

CORES = list(range(NC))



def _alibi_slopes(n):
    import math
    def pow2_slopes(n):
        start = 2.0 ** (-2.0 ** (-(math.log2(n) - 3)))
        return [start * start ** i for i in range(n)]
    if math.log2(n).is_integer():
        return pow2_slopes(n)
    cp2 = 2 ** math.floor(math.log2(n))
    return pow2_slopes(cp2) + _alibi_slopes(2 * cp2)[0::2][:n - cp2]


def _cm(v):
    return np.ascontiguousarray(np.asarray(v, np.float32).reshape(-1, 128).T)


def _fm_shard(flat, c):
    return np.ascontiguousarray(flat[c * T:(c + 1) * T].T)


MASK_POS = ([(0, kt) for kt in range(4)] + [(1, kt) for kt in range(4, 8)]
            + [(0, kt) for kt in range(8, 12)] + [(1, kt) for kt in range(12, 16)])
SKIP_POS = {(0, kt) for kt in range(12, 16)}


def _attn_consts():
    slopes = np.array(_alibi_slopes(NH), np.float64)
    dk = np.arange(128, dtype=np.float64)
    kb_pc, qb_pc, masks_pc = [], [], []
    for c in range(NC):
        odd = c % 2 == 1
        off = 1024.0 if odd else 0.0
        kb = np.zeros((128, NH * 16), np.float64)
        for h in range(NH):
            for kt in range(16):
                kg = kt * 128.0 + dk if kt < 8 else (kt - 8) * 128.0 + off + dk
                v = slopes[h] * kg
                if (not odd) and kt < 8:
                    v = np.full(128, KILL)
                kb[:, h * 16 + kt] = v
        kb_pc.append(kb.astype(np.float32))

        q_glob = np.arange(T, dtype=np.float64) + off
        qb = np.zeros((NH, 2, 2, 512), np.float64)
        for h in range(NH):
            for qt in range(2):
                v = -128.0 * slopes[h] * (q_glob[qt * 512:(qt + 1) * 512] + 64.0)
                hi = v.astype(BF).astype(np.float64)
                lo = (v - hi).astype(BF).astype(np.float64)
                qb[h, qt, 0] = hi
                qb[h, qt, 1] = lo
        qb_pc.append(qb.astype(BF))

        q_own = np.arange(T, dtype=np.float64)
        masks = np.zeros((16, 128, 512), np.float64)
        for i, (qt, kt) in enumerate(MASK_POS):
            q = q_own[qt * 512:(qt + 1) * 512]
            if kt < 8:
                masks[i] = FBIG if odd else np.where(
                    (kt * 128 + np.arange(128.0))[:, None] <= q[None, :], FBIG, 0.0)
            else:
                k = (kt - 8) * 128 + np.arange(128.0)
                masks[i] = np.where(k[:, None] <= q[None, :], FBIG, 0.0)
        masks_pc.append(masks.astype(BF))
    return kb_pc, qb_pc, masks_pc



class P:

    def __init__(self):
        self.nc = bass.Bass("TRN2", target_bir_lowering=False)
        self.stack = contextlib.ExitStack()
        self.sems = {}
        self.cnt = {}

    def sem(self, name):
        if name not in self.sems:
            self.sems[name] = self.stack.enter_context(self.nc.semaphore(name))
            self.cnt[name] = 0
        return self.sems[name]

    def inc(self, ins, name, by=1):
        ins.then_inc(self.sem(name), by)
        self.cnt[name] += by
        return self.cnt[name]

    def sbuf(self, name, shape, dt):
        return self.stack.enter_context(self.nc.sbuf_tensor(name, shape, dt))

    def psum(self, name, shape, dt=F32):
        return self.stack.enter_context(self.nc.psum_tensor(name, shape, dt))

    def scope(self):
        return self.stack.enter_context(contextlib.ExitStack())

    def sbuf_in(self, scope, name, shape, dt):
        return scope.enter_context(self.nc.sbuf_tensor(name, shape, dt))

    def chain(self, eng, ins, name="chn"):
        v = self.inc(ins, name, 1)
        eng.wait_ge(self.sem(name), v)


def chunked(ap, p=128):
    return ap.rearrange("(c p) f -> p c f", p=p)



CAST_SHARDS = {"W0aT": (D, 4096), "W0bT": (4096, D), "W1aT": (D, 4096),
               "W1bT": (4096, D), "Wck": (D, 8192), "Wcv": (8192, D)}
QN_TOT = float(D * F1)


def build_prep():
    p = P()
    nc = p.nc
    w1 = nc.dram_tensor("w1T_s", [256, F1], F32, kind="ExternalInput").ap()
    w2 = nc.dram_tensor("w2T_s", [256, F1], F32, kind="ExternalInput").ap()
    w3 = nc.dram_tensor("w3T_s", [F1, 256], F32, kind="ExternalInput").ap()
    q1 = nc.dram_tensor("q1T_s", [256, F1], BF16, kind="ExternalOutput").ap()
    q2 = nc.dram_tensor("q2T_s", [256, F1], BF16, kind="ExternalOutput").ap()
    q3 = nc.dram_tensor("q3T_s", [F1, 256], BF16, kind="ExternalOutput").ap()
    cin, cout = {}, {}
    for nm, (r, cdim) in CAST_SHARDS.items():
        cin[nm] = nc.dram_tensor(nm + "_s", [r // NC, cdim], F32, kind="ExternalInput").ap()
        cout[nm] = nc.dram_tensor(nm + "16_s", [r // NC, cdim], BF16, kind="ExternalOutput").ap()

    dbg_thr = nc.dram_tensor("dbg_thr", [128, 8], F32, kind="ExternalOutput").ap()
    dbg_rows = nc.dram_tensor("dbg_rows", [3, 8], F32, kind="ExternalOutput").ap()
    cc_in = nc.dram_tensor("cc_in", [1, 8], F32).ap()
    cc_out = nc.dram_tensor("cc_out", [1, 8], F32).ap()
    colb = nc.dram_tensor("colb", [1, 512], F32).ap()
    rowb = nc.dram_tensor("rowb", [1, 8], F32).ap()

    wv = {"w1": (chunked(w1), chunked(q1), F1), "w2": (chunked(w2), chunked(q2), F1),
          "w3": (chunked(w3), chunked(q3), 256)}
    TILES = []
    for nm in ("w1", "w2", "w3"):
        iv, ov, wid = wv[nm]
        tw = wid // 4
        for i in range(4):
            TILES.append((nm, iv[:, :, i * tw:(i + 1) * tw], ov[:, :, i * tw:(i + 1) * tw]))

    CT = []
    for nm, (r, cdim) in CAST_SHARDS.items():
        nch = (r // NC) // 128
        na = cdim // 2048
        ivf = chunked(cin[nm]).rearrange("p c (a b) -> p c a b", b=2048)
        ovf = chunked(cout[nm]).rearrange("p c (a b) -> p c a b", b=2048)
        if nch * na == 4:
            CT.append((ivf, ovf, (nch, na)))
        elif na == 4:
            for i in range(2):
                CT.append((ivf[:, :, i * 2:(i + 1) * 2], ovf[:, :, i * 2:(i + 1) * 2], (nch, 2)))
        else:
            for i in range(2):
                CT.append((ivf[:, i * 4:(i + 1) * 4], ovf[:, i * 4:(i + 1) * 4], (4, 1)))

    with p.stack:
        wt = p.sbuf("wt", [128, 2, 2, 1408], F32)
        wt3 = p.sbuf("wt3", [128, 2, 44, 64], F32)
        acc = p.sbuf("acc", [128, 12], F32)
        accw = p.sbuf("accw", [128, 4], F32)
        ta = p.sbuf("ta", [1, 8], F32)
        rr = p.sbuf("rr", [1, 512], F32)
        prow = p.sbuf("prow", [1, 8], F32)
        trow = p.sbuf("trow", [1, 8], F32)
        t2row = p.sbuf("t2row", [1, 8], F32)
        thr = p.sbuf("thr", [128, 8], F32)
        qa = p.sbuf("qa", [128, 2, 1408], F32)
        qa3 = p.sbuf("qa3", [128, 44, 64], F32)
        qo = p.sbuf("qo", [128, 2, 2, 1408], BF16)
        qo3 = p.sbuf("qo3", [128, 2, 44, 64], BF16)
        cb = p.sbuf("cb", [128, 2, 4, 2048], BF16)

        def buf(nm, i):
            return wt3[:, i % 2] if nm == "w3" else wt[:, i % 2]

        def qbuf(nm, i):
            return qo3[:, i % 2] if nm == "w3" else qo[:, i % 2]

        def abuf(nm):
            return qa3 if nm == "w3" else qa

        with nc.Block() as block:

            @block.sync
            def _(sync):
                for i, (nm, iv, ov) in enumerate(TILES):
                    if i >= 2:
                        sync.wait_ge(p.sem("gcons"), i - 1)
                    p.inc(sync.dma_start(out=buf(nm, i), in_=iv), f"ld1_{i % 2}", 16)
                sync.wait_ge(p.sem("gcons"), 12)
                for i, (nm, iv, ov) in enumerate(TILES):
                    if i >= 2:
                        sync.wait_ge(p.sem("qdone"), i - 1)
                    p.inc(sync.dma_start(out=buf(nm, i), in_=iv), f"ld3_{i % 2}", 16)

            @block.vector
            def _(vector):
                for i, (nm, iv, ov) in enumerate(TILES):
                    vector.wait_ge(p.sem(f"ld1_{i % 2}"), 16 * (i // 2 + 1))
                    ins = vector.tensor_reduce(acc[:, i:i + 1], buf(nm, i)[:], AX.XY, ALU.add,
                                               apply_absolute_value=True)
                    p.inc(ins, "gcons", 1)
                vector.wait_ge(p.sem("gcons"), 12)
                for wi in range(3):
                    vector.tensor_reduce(accw[:, wi:wi + 1], acc[:, 4 * wi:4 * wi + 4], AX.X, ALU.add)
                p.inc(vector.tensor_copy(accw[:, 3:4], acc[:, 0:1]), "accw", 1)
                vector.wait_ge(p.sem("rowld"), 16)
                for wi in range(3):
                    ins = vector.tensor_reduce(prow[0:1, wi:wi + 1], rr[0:1, wi:512:4], AX.X, ALU.add)
                    p.inc(ins, "vch", 1)
                    vector.wait_ge(p.sem("vch"), p.cnt["vch"])
                p.inc(vector.tensor_copy(prow[0:1, 3:8], rr[0:1, 3:8]), "partials", 1)
                vector.wait_ge(p.sem("trld"), 16)
                ins = vector.tensor_scalar(ta[0:1, 0:3], trow[0:1, 0:3], 1.0 / QN_TOT, 1e-8, ALU.mult, ALU.add)
                p.inc(ins, "vch", 1)
                vector.wait_ge(p.sem("vch"), p.cnt["vch"])
                ins = vector.tensor_scalar(t2row[0:1, 0:3], ta[0:1, 0:3], 0.5, None, ALU.mult)
                p.inc(ins, "vch", 1)
                vector.wait_ge(p.sem("vch"), p.cnt["vch"])
                vector.tensor_scalar(t2row[0:1, 4:7], t2row[0:1, 0:3], -1.0, None, ALU.mult)
                ins = vector.tensor_copy(t2row[0:1, 3:4], t2row[0:1, 0:1])
                p.inc(ins, "vch", 1)
                vector.wait_ge(p.sem("vch"), p.cnt["vch"])
                p.inc(vector.tensor_copy(t2row[0:1, 7:8], t2row[0:1, 0:1]), "thrmath", 1)
                for i, (nm, iv, ov) in enumerate(TILES):
                    vector.wait_ge(p.sem(f"ld3_{i % 2}"), 16 * (i // 2 + 1))
                    if i == 0:
                        vector.wait_ge(p.sem("thrld"), 16)
                    if i >= 2:
                        vector.wait_ge(p.sem(f"qst_{i % 2}"), 16 * (i // 2))
                    wi = i // 4
                    src, dst, sc = buf(nm, i), qbuf(nm, i), abuf(nm)
                    if i >= 1:
                        vector.wait_ge(p.sem("qdone"), i)
                    p.chain(vector, vector.tensor_scalar(sc[:], src[:], thr[:, 4 + wi:5 + wi],
                                                         None, ALU.is_lt), "vch")
                    ins = vector.scalar_tensor_tensor(dst[:], src[:], thr[:, wi:wi + 1], sc[:],
                                                      ALU.is_gt, ALU.subtract)
                    p.inc(ins, "qdone", 1)

            @block.scalar
            def _(scalar):
                scalar.wait_ge(p.sem("accw"), 1)
                p.inc(scalar.dma_start(out=colb.rearrange("o (pp w) -> pp (o w)", w=4), in_=accw[:]), "cold", 16)
                scalar.wait_ge(p.sem("cold"), 16)
                p.inc(scalar.dma_start(out=rr[0:1, :], in_=colb[0:1, :]), "rowld", 16)
                scalar.wait_ge(p.sem("partials"), 1)
                p.inc(scalar.dma_start(out=cc_in[0:1, :], in_=prow[0:1, :]), "ccin", 16)
                scalar.wait_ge(p.sem("ccdone"), 1)
                p.inc(scalar.dma_start(out=trow[0:1, :], in_=cc_out[0:1, :]), "trld", 16)
                scalar.wait_ge(p.sem("thrmath"), 1)
                p.inc(scalar.dma_start(out=rowb[0:1, :], in_=t2row[0:1, :]), "rbst", 16)
                scalar.wait_ge(p.sem("rbst"), 16)
                p.inc(scalar.dma_start(out=thr[:], in_=rowb.broadcast_to((128, 8))), "thrld", 16)
                for i, (nm, iv, ov) in enumerate(TILES):
                    scalar.wait_ge(p.sem("qdone"), i + 1)
                    p.inc(scalar.dma_start(out=ov, in_=qbuf(nm, i)), f"qst_{i % 2}", 16)
                for j, (ivf, ovf, (cc, aa)) in enumerate(CT):
                    scalar.wait_ge(p.sem(f"cld_{j % 2}"), 16 * (j // 2 + 1))
                    cbv = cb[:, j % 2].rearrange("p (c a) b -> p c a b", c=cc)
                    p.inc(scalar.dma_start(out=ovf, in_=cbv), f"cst_{j % 2}", 16)
                scalar.wait_ge(p.sem("qst_0"), 16 * 6)
                scalar.wait_ge(p.sem("qst_1"), 16 * 6)
                scalar.wait_ge(p.sem("cst_0"), 16 * ((len(CT) + 1) // 2))
                scalar.wait_ge(p.sem("cst_1"), 16 * (len(CT) // 2))
                scalar.dma_start(out=dbg_thr[:], in_=thr[:]).then_inc(p.sem("dbg"), 16)
                scalar.dma_start(out=dbg_rows[0:1, :], in_=prow[0:1, :]).then_inc(p.sem("dbg"), 16)
                scalar.dma_start(out=dbg_rows[1:2, :], in_=trow[0:1, :]).then_inc(p.sem("dbg"), 16)
                scalar.dma_start(out=dbg_rows[2:3, :], in_=t2row[0:1, :]).then_inc(p.sem("dbg"), 16)
                scalar.wait_ge(p.sem("dbg"), 64)

            @block.gpsimd
            def _(gpsimd):
                gpsimd.wait_ge(p.sem("ccin"), 16)
                p.inc(gpsimd.collective_compute(
                    "AllReduce", ALU.add, replica_groups=[CORES],
                    ins=[cc_in.opt()], outs=[cc_out.opt()]), "ccdone", 1)
                for j, (ivf, ovf, (cc, aa)) in enumerate(CT):
                    if j >= 2:
                        gpsimd.wait_ge(p.sem(f"cst_{j % 2}"), 16 * (j // 2))
                    cbv = cb[:, j % 2].rearrange("p (c a) b -> p c a b", c=cc)
                    p.inc(gpsimd.dma_start(out=cbv, in_=ivf), f"cld_{j % 2}", 16)

    return p.nc


def _prep_inputs(inp):
    w1T = np.ascontiguousarray(inp["L0_ffn_w1"].T)
    w2T = np.ascontiguousarray(inp["L0_ffn_w2"].T)
    w3T = np.ascontiguousarray(inp["L0_ffn_w3"].T)
    casts = {"W0aT": np.ascontiguousarray(inp["L1_ffn_W0a"].T),
             "W0bT": np.ascontiguousarray(inp["L1_ffn_W0b"].T),
             "W1aT": np.ascontiguousarray(inp["L1_ffn_W1a"].T),
             "W1bT": np.ascontiguousarray(inp["L1_ffn_W1b"].T),
             "Wck": np.ascontiguousarray(inp["L1_attn_Wck"]),
             "Wcv": np.ascontiguousarray(inp["L1_attn_Wcv"])}
    maps = []
    for c in range(NC):
        m = {
            "w1T_s": np.ascontiguousarray(w1T[c * 256:(c + 1) * 256]),
            "w2T_s": np.ascontiguousarray(w2T[c * 256:(c + 1) * 256]),
            "w3T_s": np.ascontiguousarray(w3T[:, c * 256:(c + 1) * 256]),
        }
        for nm, arr in casts.items():
            rs = arr.shape[0] // NC
            m[nm + "_s"] = np.ascontiguousarray(arr[c * rs:(c + 1) * rs])
        maps.append(m)
    return maps


def _prep_gather(results):
    out = {}
    out["q1T"] = np.vstack([results[c]["q1T_s"] for c in range(NC)])
    out["q2T"] = np.vstack([results[c]["q2T_s"] for c in range(NC)])
    out["q3T"] = np.ascontiguousarray(np.hstack([results[c]["q3T_s"] for c in range(NC)]))
    for nm in CAST_SHARDS:
        out[nm + "16"] = np.vstack([results[c][nm + "16_s"] for c in range(NC)])
    return out



def emit_ln(p, block, pref, xs, y, ttw, ones_bf, b1, b2, mb, rb, tmp, tmp2, xbf, xsq,
            ps_s, ps_q, eps_ap, srow, sqrow, r1, r2, sd, rstd,
            gp_wait=(), sc_wait=(), te_wait=(), ve_wait=(), gains=None, nch=CD,
            prev=None):

    @block.gpsimd
    def _(gp):
        for s, v in gp_wait:
            gp.wait_ge(p.sem(s), v)
        for c in range(nch):
            if c >= 2:
                gp.wait_ge(p.sem(pref + "mm"), c - 1)
            p.inc(gp.tensor_copy(xbf[:, c % 2], xs[:, c]), pref + "xb", 1)

    @block.scalar
    def _(sc):
        for s, v in sc_wait:
            sc.wait_ge(p.sem(s), v)
        for c in range(nch):
            if c >= 2:
                sc.wait_ge(p.sem(pref + "mm"), c - 1)
            p.inc(sc.activation(xsq[:, c % 2], xs[:, c], AF.Square), pref + "xq", 1)
        sc.wait_ge(p.sem(pref + "mm"), nch)
        if prev is not None:
            sc.wait_ge(p.sem(prev + "bst"), 32)
        p.inc(sc.activation(srow[0:1, :], ps_s[0:1, :], AF.Identity, scale=1.0 / D), pref + "sr", 1)
        p.inc(sc.activation(sqrow[0:1, :], ps_q[0:1, :], AF.Identity, scale=1.0 / D), pref + "sr", 1)
        sc.wait_ge(p.sem(pref + "var"), 1)
        p.inc(sc.activation(sd[0:1, :], r2[0:1, :], AF.Sqrt, bias=eps_ap), pref + "sd", 1)
        sc.wait_ge(p.sem(pref + "rstd"), 1)
        if prev is not None:
            sc.wait_ge(p.sem(prev + "bld"), 32)
        p.inc(sc.dma_start(out=b1[0:1, :], in_=srow[0:1, :]), pref + "bst", 16)
        p.inc(sc.dma_start(out=b2[0:1, :], in_=rstd[0:1, :]), pref + "bst", 16)
        sc.wait_ge(p.sem(pref + "bst"), 32)
        p.inc(sc.dma_start(out=mb[:], in_=b1.broadcast_to((128, ttw))), pref + "bld", 16)
        p.inc(sc.dma_start(out=rb[:], in_=b2.broadcast_to((128, ttw))), pref + "bld", 16)

    @block.tensor
    def _(te):
        for s, v in te_wait:
            te.wait_ge(p.sem(s), v)
        for c in range(nch):
            te.wait_ge(p.sem(pref + "xb"), c + 1)
            te.wait_ge(p.sem(pref + "xq"), c + 1)
            te.matmul(ps_s[0:1, :], ones_bf[:, 0:1], xbf[:, c % 2],
                      start=(c == 0), stop=(c == nch - 1))
            ins = te.matmul(ps_q[0:1, :], ones_bf[:, 0:1], xsq[:, c % 2],
                            start=(c == 0), stop=(c == nch - 1))
            p.inc(ins, pref + "mm", 1)

    @block.vector
    def _(ve):
        for s, v in ve_wait:
            ve.wait_ge(p.sem(s), v)
        ve.wait_ge(p.sem(pref + "sr"), 2)
        p.inc(ve.tensor_mul(r1[0:1, :], srow[0:1, :], srow[0:1, :]), pref + "m2", 1)
        ve.wait_ge(p.sem(pref + "m2"), 1)
        p.inc(ve.tensor_sub(r2[0:1, :], sqrow[0:1, :], r1[0:1, :]), pref + "var", 1)
        ve.wait_ge(p.sem(pref + "sd"), 1)
        p.inc(ve.reciprocal(rstd[0:1, :], sd[0:1, :]), pref + "rstd", 1)
        ve.wait_ge(p.sem(pref + "bld"), 32)
        for c in range(nch):
            p.chain(ve, ve.tensor_sub(tmp[:, c % 2], xs[:, c], mb[:]), pref + "ch")
            if gains is None:
                ins = ve.tensor_mul(y[:, c], tmp[:, c % 2], rb[:])
            else:
                g_cm, b_cm = gains
                p.chain(ve, ve.tensor_mul(tmp2[:, c % 2], tmp[:, c % 2], rb[:]), pref + "ch")
                ins = ve.tensor_scalar(y[:, c], tmp2[:, c % 2], g_cm[:, c:c + 1],
                                       b_cm[:, c:c + 1], ALU.mult, ALU.add)
            p.inc(ins, pref + "y", 1)



MG_B = 256
NGRP_B = F1 // MG_B
FCH = F1 // 128


def build_l0ffn():
    p = P()
    nc = p.nc
    x = nc.dram_tensor("x", [D, T], F32, kind="ExternalInput").ap()
    q1T = nc.dram_tensor("q1T", [D, F1], BF16, kind="ExternalInput").ap()
    q2T = nc.dram_tensor("q2T", [D, F1], BF16, kind="ExternalInput").ap()
    q3T = nc.dram_tensor("q3T", [F1, D], BF16, kind="ExternalInput").ap()
    ones_in = nc.dram_tensor("ones_bf", [128, 1], BF16, kind="ExternalInput").ap()
    cst_in = nc.dram_tensor("cstf", [1, 8], F32, kind="ExternalInput").ap()
    xo = nc.dram_tensor("xo", [D, T], F32, kind="ExternalOutput").ap()
    b1 = nc.dram_tensor("b1", [1, 512], F32).ap()
    b2 = nc.dram_tensor("b2", [1, 512], F32).ap()

    xv = chunked(x)
    xov = chunked(xo)
    q1v = chunked(q1T)
    q2v = chunked(q2T)
    q3v = chunked(q3T)

    with p.stack:
        ones_bf = p.sbuf("ones_sb", [128, 1], BF16)
        cst = p.sbuf("cst", [1, 8], F32)
        xs = p.sbuf("xs", [128, CD, 512], F32)
        yb = p.sbuf("yb", [128, CD, 512], BF16)
        tmp = p.sbuf("tmp", [128, 2, 512], F32)
        xbf = p.sbuf("xbf", [128, 2, 512], BF16)
        xsq = p.sbuf("xsq", [128, 2, 512], BF16)
        mb = p.sbuf("mb", [128, 512], F32)
        rb = p.sbuf("rb", [128, 512], F32)
        rows = p.sbuf("rows", [1, 6, 512], F32)
        w1s = p.sbuf("w1s", [128, 2, CD, MG_B], BF16)
        w2s = p.sbuf("w2s", [128, 2, CD, MG_B], BF16)
        hb = p.sbuf("hb", [128, FCH, 512], BF16)
        sgt = p.sbuf("sgt", [128, 2, 512], BF16)
        q3s = p.sbuf("q3s", [128, 2, 1024], BF16)
        ob = p.sbuf("ob", [128, 8, 512], F32)
        ps = [p.psum(f"ps{i}", [128, 512]) for i in range(8)]
        srow, sqrow, r1, r2, sd, rstd = (rows[:, i] for i in range(6))

        with nc.Block() as block:
            @block.sync
            def _(sync):
                sync.dma_start(out=ones_bf[:], in_=ones_in[:]).then_inc(p.sem("lone"), 16)
                p.inc(sync.dma_start(out=cst[0:1, :], in_=cst_in[0:1, :]), "lone", 16)
                for tt in range(2):
                    tsl = slice(tt * 512, (tt + 1) * 512)
                    if tt > 0:
                        sync.wait_ge(p.sem("obd"), 16 * tt)
                    p.inc(sync.dma_start(out=xs[:], in_=xv[:, :, tsl]), "lx", 16)
                    for g in range(NGRP_B):
                        G = tt * NGRP_B + g
                        if G >= 2:
                            sync.wait_ge(p.sem("mmh"), 2 * (G - 1))
                        msl = slice(g * MG_B, (g + 1) * MG_B)
                        p.inc(sync.dma_start(out=w1s[:, G % 2], in_=q1v[:, :, msl]), f"lw1_{G % 2}", 16)
                        p.inc(sync.dma_start(out=w2s[:, G % 2], in_=q2v[:, :, msl]), f"lw2_{G % 2}", 16)
                    for dg in range(2):
                        DG = tt * 2 + dg
                        for f in range(FCH):
                            FD = DG * FCH + f
                            if FD >= 2:
                                sync.wait_ge(p.sem("q3c"), FD - 1)
                            dsl = slice(dg * 1024, (dg + 1) * 1024)
                            p.inc(sync.dma_start(out=q3s[:, FD % 2], in_=q3v[:, f, dsl]), f"lq3_{FD % 2}", 16)

            for tt in range(2):
                emit_ln(p, block, f"ln{tt}_", xs, yb, 512, ones_bf, b1, b2, mb, rb,
                        tmp, None, xbf, xsq, ps[4][0:1, :], ps[5][0:1, :], cst[0:1, 0:1],
                        srow, sqrow, r1, r2, sd, rstd,
                        gp_wait=[("lx", 16 * (tt + 1))],
                        sc_wait=[("lx", 16 * (tt + 1)), ("lone", 32)],
                        te_wait=[("lone", 32), ("obd", 16 * tt)],
                        ve_wait=[("q3c", 2 * FCH * tt)])

                tsl = slice(tt * 512, (tt + 1) * 512)

                @block.tensor
                def _(te, tt=tt):
                    for g in range(NGRP_B):
                        G = tt * NGRP_B + g
                        te.wait_ge(p.sem(f"lw1_{G % 2}"), 16 * (G // 2 + 1))
                        te.wait_ge(p.sem(f"lw2_{G % 2}"), 16 * (G // 2 + 1))
                        if g == 0:
                            te.wait_ge(p.sem(f"ln{tt}_y"), CD)
                        for m in range(2):
                            MH = G * 2 + m
                            if MH >= 2:
                                te.wait_ge(p.sem("hdone"), MH - 1)
                            pa, pb_ = ps[2 * (MH % 2)], ps[2 * (MH % 2) + 1]
                            msl = slice(m * 128, (m + 1) * 128)
                            for c in range(CD):
                                te.matmul(pa[:], w1s[:, G % 2, c, msl], yb[:, c],
                                          start=(c == 0), stop=(c == CD - 1))
                            for c in range(CD):
                                ins = te.matmul(pb_[:], w2s[:, G % 2, c, msl], yb[:, c],
                                                start=(c == 0), stop=(c == CD - 1))
                            p.inc(ins, "mmh", 1)

                @block.scalar
                def _(sc, tt=tt):
                    for g in range(NGRP_B):
                        for m in range(2):
                            MH = (tt * NGRP_B + g) * 2 + m
                            sc.wait_ge(p.sem("mmh"), MH + 1)
                            if MH >= 2:
                                sc.wait_ge(p.sem("hdone"), MH - 1)
                            p.inc(sc.activation(sgt[:, MH % 2], ps[2 * (MH % 2)][:], AF.Sigmoid),
                                  "sig", 1)

                @block.vector
                def _(ve, tt=tt):
                    for g in range(NGRP_B):
                        for m in range(2):
                            MH = (tt * NGRP_B + g) * 2 + m
                            ve.wait_ge(p.sem("sig"), MH + 1)
                            col = g * 2 + m
                            p.inc(ve.tensor_mul(hb[:, col], sgt[:, MH % 2],
                                                ps[2 * (MH % 2) + 1][:]), "hdone", 1)

                @block.tensor
                def _(te, tt=tt):
                    for dg in range(2):
                        DG = tt * 2 + dg
                        te.wait_ge(p.sem("hdone"), 44 * (tt + 1))
                        te.wait_ge(p.sem("obd"), 8 * DG)
                        for f in range(FCH):
                            FD = DG * FCH + f
                            te.wait_ge(p.sem(f"lq3_{FD % 2}"), 16 * (FD // 2 + 1))
                            for dout in range(8):
                                ins = te.matmul(ps[dout][:],
                                                q3s[:, FD % 2, dout * 128:(dout + 1) * 128],
                                                hb[:, f], start=(f == 0), stop=(f == FCH - 1))
                            p.inc(ins, "q3c", 1)

                @block.vector
                def _(ve, tt=tt):
                    for dg in range(2):
                        DG = tt * 2 + dg
                        ve.wait_ge(p.sem("q3c"), FCH * (DG + 1))
                        ve.wait_ge(p.sem("obst"), 16 * DG)
                        for dout in range(8):
                            ch = dg * 8 + dout
                            p.inc(ve.scalar_tensor_tensor(ob[:, dout], ps[dout][:], 0.5,
                                                          xs[:, ch], ALU.mult, ALU.add),
                                  "obd", 1)

                @block.scalar
                def _(sc, tt=tt):
                    for dg in range(2):
                        DG = tt * 2 + dg
                        sc.wait_ge(p.sem("obd"), 8 * (DG + 1))
                        p.inc(sc.dma_start(out=xov[:, dg * 8:(dg + 1) * 8, tsl], in_=ob[:]),
                              "obst", 16)

            @block.scalar
            def _(sc):
                sc.wait_ge(p.sem("obst"), 16 * 4)

    return p.nc



ATT_KTS = {0: list(range(12)), 1: list(range(16))}


def build_l0attn():
    p = P()
    nc = p.nc
    x = nc.dram_tensor("x", [D, T], F32, kind="ExternalInput").ap()
    xfh = nc.dram_tensor("xfh", [D, T], F32, kind="ExternalInput").ap()
    wqkv = nc.dram_tensor("wqkv", [D, 3 * D], F32, kind="ExternalInput").ap()
    bqkv_cm = nc.dram_tensor("bqkv_cm", [128, 48], F32, kind="ExternalInput").ap()
    bqkv_row = nc.dram_tensor("bqkv_row", [1, 3 * D], F32, kind="ExternalInput").ap()
    wproj = nc.dram_tensor("wproj", [D, D], F32, kind="ExternalInput").ap()
    bproj_cm = nc.dram_tensor("bproj_cm", [128, CD], F32, kind="ExternalInput").ap()
    kbias_in = nc.dram_tensor("kbias", [128, NH * 16], F32, kind="ExternalInput").ap()
    qbias_in = nc.dram_tensor("qbias", [NH, 2, 2, 512], BF16, kind="ExternalInput").ap()
    masks_in = nc.dram_tensor("masks", [16, 128, 512], BF16, kind="ExternalInput").ap()
    ones_in = nc.dram_tensor("ones_bf", [128, 1], BF16, kind="ExternalInput").ap()
    ones2_in = nc.dram_tensor("ones2", [2, 128], BF16, kind="ExternalInput").ap()
    cst_in = nc.dram_tensor("cstf", [1, 8], F32, kind="ExternalInput").ap()
    xo = nc.dram_tensor("xo", [D, T], F32, kind="ExternalOutput").ap()
    b1 = nc.dram_tensor("b1", [1, 512], F32).ap()
    b2 = nc.dram_tensor("b2", [1, 512], F32).ap()
    vdram = nc.dram_tensor("vdram", [2048, D], BF16).ap()
    adram = nc.dram_tensor("adram", [D, T], BF16).ap()

    xv, xfv = chunked(x), chunked(xfh)
    wqv = chunked(wqkv)
    wpv = chunked(wproj)
    vdv = chunked(vdram)
    adv = chunked(adram)
    xov = chunked(xo)

    LKT = [len(ATT_KTS[hq % 2]) for hq in range(32)]
    S_BASE = [0]
    for hq in range(32):
        S_BASE.append(S_BASE[-1] + LKT[hq])
    MASK_IDX = {pos: i for i, pos in enumerate(MASK_POS)}
    MK = [sum(1 for kt in ATT_KTS[hq % 2] if (hq % 2, kt) in MASK_IDX) for hq in range(32)]
    M_BASE = [0]
    for hq in range(32):
        M_BASE.append(M_BASE[-1] + MK[hq])

    with p.stack:
        masks = p.sbuf("masks_s", [128, 16, 512], BF16)
        kb = p.sbuf("kb", [128, NH * 16], F32)
        ones_bf = p.sbuf("ones_sb", [128, 1], BF16)
        ones2 = p.sbuf("ones2_sb", [2, 128], BF16)
        cst = p.sbuf("cst", [1, 8], F32)
        bq = p.sbuf("bq", [128, 48], F32)
        bvr = p.sbuf("bvr", [1, 2048], BF16)
        bp = p.sbuf("bp", [128, CD], F32)
        xln_o = p.sbuf("xln_o", [128, CD, T], BF16)
        xln_f = p.sbuf("xln_f", [128, CD, T], BF16)
        rows = p.sbuf("rows", [1, 6, 512], F32)
        srow, sqrow, r1, r2, sd, rstd = (rows[:, i] for i in range(6))
        ps = [p.psum(f"ps{i}", [128, 512]) for i in range(8)]

        NCON = 8
        with nc.sbuf_tensor("bqr", [1, 3 * D], F32) as bqr, nc.Block() as block:
            @block.sync
            def _(sync):
                for dst, srcv in ((masks[:], masks_in.rearrange("i p f -> p i f")),
                                  (kb[:], kbias_in[:]),
                                  (ones_bf[:], ones_in[:]), (ones2[:], ones2_in[:]),
                                  (cst[0:1, :], cst_in[0:1, :]),
                                  (bq[:], bqkv_cm[:]), (bqr[0:1, :], bqkv_row[0:1, :]),
                                  (bp[:], bproj_cm[:])):
                    p.inc(sync.dma_start(out=dst, in_=srcv), "lcst", 16)

            @block.vector
            def _(ve):
                ve.wait_ge(p.sem("lcst"), 16 * NCON)
                p.inc(ve.tensor_copy(bvr[0:1, :], bqr[0:1, 4096:6144]), "bvrd", 1)

        with (nc.sbuf_tensor("xbuf", [128, CD, 512], F32) as xbuf,
              nc.sbuf_tensor("lntmp", [128, 2, 512], F32) as lntmp,
              nc.sbuf_tensor("lnxb", [128, 2, 512], BF16) as lnxb,
              nc.sbuf_tensor("lnxq", [128, 2, 512], BF16) as lnxq,
              nc.sbuf_tensor("lnmb", [128, 512], F32) as lnmb,
              nc.sbuf_tensor("lnrb", [128, 512], F32) as lnrb):
            passes = [(xv, xln_o, 0), (xv, xln_o, 1), (xfv, xln_f, 0), (xfv, xln_f, 1)]
            for k, (srcv, dstv, tt) in enumerate(passes):
                tsl = slice(tt * 512, (tt + 1) * 512)
                with nc.Block() as block:
                    @block.sync
                    def _(sync, k=k, srcv=srcv, tsl=tsl):
                        if k > 0:
                            sync.wait_ge(p.sem(f"ln{k - 1}_y"), CD)
                        p.inc(sync.dma_start(out=xbuf[:], in_=srcv[:, :, tsl]), f"lx{k}", 16)
                    emit_ln(p, block, f"ln{k}_", xbuf, dstv[:, :, tsl], 512, ones_bf,
                            b1, b2, lnmb, lnrb, lntmp, None, lnxb, lnxq,
                            ps[4][0:1, :], ps[5][0:1, :], cst[0:1, 0:1],
                            srow, sqrow, r1, r2, sd, rstd,
                            gp_wait=[(f"lx{k}", 16)],
                            sc_wait=[(f"lx{k}", 16), ("lcst", 16 * NCON)],
                            te_wait=[("lcst", 16 * NCON)],
                            prev=(f"ln{k - 1}_" if k > 0 else None))

        with (nc.sbuf_tensor("wvs", [128, 2, CD, 512], BF16) as wvs,
              nc.sbuf_tensor("vt", [128, 2, 512], BF16) as vt):
            with nc.Block() as block:
                @block.gpsimd
                def _(gp):
                    gp.wait_ge(p.sem("ln3_y"), CD)
                    for gn in range(4):
                        if gn >= 2:
                            gp.wait_ge(p.sem("vmm"), 16 * (gn - 1))
                        csl = slice(4096 + gn * 512, 4096 + (gn + 1) * 512)
                        p.inc(gp.dma_start(out=wvs[:, gn % 2], in_=wqv[:, :, csl]),
                              f"lwv{gn % 2}", 16)

                @block.tensor
                def _(te):
                    te.wait_ge(p.sem("ln1_y"), CD)
                    te.wait_ge(p.sem("ln3_y"), CD)
                    te.wait_ge(p.sem("bvrd"), 1)
                    vm = 0
                    for gn in range(4):
                        te.wait_ge(p.sem(f"lwv{gn % 2}"), 16 * (gn // 2 + 1))
                        for s in range(16):
                            pb = ps[vm % 2]
                            if vm >= 2:
                                te.wait_ge(p.sem("vev"), vm - 1)
                            xsrc = xln_f if s < 8 else xln_o
                            ssl = slice((s % 8) * 128, (s % 8 + 1) * 128)
                            for c in range(CD):
                                te.matmul(pb[:], xsrc[:, c, ssl], wvs[:, gn % 2, c, :],
                                          start=(c == 0), stop=False)
                            ins = te.matmul(pb[:], ones2[0:1, :],
                                            bvr[0:1, gn * 512:(gn + 1) * 512],
                                            start=False, stop=True)
                            p.inc(ins, "vmm", 1)
                            vm += 1

                @block.scalar
                def _(sc):
                    vm = 0
                    for gn in range(4):
                        for s in range(16):
                            sc.wait_ge(p.sem("vmm"), vm + 1)
                            if vm >= 2:
                                sc.wait_ge(p.sem(f"vst{vm % 2}"), 16 * (vm // 2))
                            p.inc(sc.activation(vt[:, vm % 2], ps[vm % 2][:], AF.Identity),
                                  "vev", 1)
                            sc.wait_ge(p.sem("vev"), vm + 1)
                            p.inc(sc.dma_start(out=vdv[:, s, gn * 512:(gn + 1) * 512],
                                               in_=vt[:, vm % 2]), f"vst{vm % 2}", 16)
                            vm += 1

        with (nc.sbuf_tensor("wks", [128, 2, CD, 128], BF16) as wks,
              nc.sbuf_tensor("wqs", [128, 2, CD, 128], BF16) as wqs,
              nc.sbuf_tensor("kh", [128, 2, 2048], BF16) as kh,
              nc.sbuf_tensor("qh", [128, 2, 1024], BF16) as qh,
              nc.sbuf_tensor("vh", [128, 1, CD, 128], BF16) as vh,
              nc.sbuf_tensor("qb", [2, 2, 512], BF16) as qb,
              nc.sbuf_tensor("pt", [128, 2, 512], BF16) as pt,
              nc.sbuf_tensor("ptm", [128, 2, 512], BF16) as ptm,
              nc.sbuf_tensor("ao", [128, 2, 512], BF16) as ao,
              nc.sbuf_tensor("avs", [128, 512], F32) as avs,
              nc.sbuf_tensor("nrmb1", [1, 2, 512], BF16) as nrmb1,
              nc.sbuf_tensor("nrmb2", [1, 2, 512], BF16) as nrmb2):
            nrm = rows[:, 0:4]
            with nc.Block() as block:
                @block.gpsimd
                def _(gp):
                    gp.wait_ge(p.sem("vst0"), 16 * 32)
                    gp.wait_ge(p.sem("vst1"), 16 * 32)
                    for h in range(NH):
                        if h >= 2:
                            gp.wait_ge(p.sem("qev"), 2 * (h - 1))
                        p.inc(gp.dma_start(out=wks[:, h % 2],
                                           in_=wqv[:, :, 2048 + h * 128:2048 + (h + 1) * 128]),
                              f"lwk{h % 2}", 16)
                        p.inc(gp.dma_start(out=wqs[:, h % 2],
                                           in_=wqv[:, :, h * 128:(h + 1) * 128]),
                              f"lwq{h % 2}", 16)

                @block.sync
                def _(sync):
                    sync.wait_ge(p.sem("vst0"), 16 * 32)
                    sync.wait_ge(p.sem("vst1"), 16 * 32)
                    for h in range(NH):
                        if h >= 1:
                            sync.wait_ge(p.sem("avdone"), 2 * h)
                        p.inc(sync.dma_start(out=vh[:, 0],
                                             in_=vdv[:, :, h * 128:(h + 1) * 128]),
                              "lvh", 16)
                        for qt in range(2):
                            hq = h * 2 + qt
                            if hq >= 2:
                                sync.wait_ge(p.sem("smm"), S_BASE[hq - 1])
                            p.inc(sync.dma_start(out=qb[:, hq % 2],
                                                 in_=qbias_in[h, qt]),
                                f"lqb{hq % 2}", 16)

                @block.tensor
                def _(te):
                    for h in range(NH):
                        te.wait_ge(p.sem(f"lwk{h % 2}"), 16 * (h // 2 + 1))
                        if h >= 2:
                            te.wait_ge(p.sem("smm"), S_BASE[2 * (h - 1)])
                        for st in range(4):
                            if st >= 2:
                                te.wait_ge(p.sem("kev"), 4 * h + st - 1)
                            xsrc = xln_f if st < 2 else xln_o
                            tslq = slice((st % 2) * 512, (st % 2 + 1) * 512)
                            for c in range(CD):
                                ins = te.matmul(ps[2 + st % 2][:], wks[:, h % 2, c, :],
                                                xsrc[:, c, tslq], start=(c == 0),
                                                stop=(c == CD - 1))
                            p.inc(ins, "kmm", 1)
                        te.wait_ge(p.sem(f"lwq{h % 2}"), 16 * (h // 2 + 1))
                        te.wait_ge(p.sem("kev"), 4 * (h + 1))
                        for qt in range(2):
                            for c in range(CD):
                                ins = te.matmul(ps[2 + qt][:], wqs[:, h % 2, c, :],
                                                xln_o[:, c, qt * 512:(qt + 1) * 512],
                                                start=(c == 0), stop=(c == CD - 1))
                            p.inc(ins, "qmm", 1)
                        te.wait_ge(p.sem("lvh"), 16 * (h + 1))
                        te.wait_ge(p.sem("qev"), 2 * (h + 1))
                        for qt in range(2):
                            hq = h * 2 + qt
                            te.wait_ge(p.sem(f"lqb{hq % 2}"), 16 * (hq // 2 + 1))
                            if hq > 0:
                                te.wait_ge(p.sem("avdone"), hq)
                            kts = ATT_KTS[qt]
                            nm = 0
                            for j, kt in enumerate(kts):
                                pb = ps[j % 2]
                                if j >= 2:
                                    te.wait_ge(p.sem("expdone"), S_BASE[hq] + j - 1)
                                te.matmul(pb[:], kh[:, h % 2, kt * 128:(kt + 1) * 128],
                                          qh[:, h % 2, qt * 512:(qt + 1) * 512],
                                          start=True, stop=False)
                                ins = te.matmul(pb[:], ones2[:, :],
                                                qb[:, hq % 2], start=False, stop=True)
                                p.inc(ins, "smm", 1)
                                masked = (qt, kt) in MASK_IDX
                                if masked:
                                    te.wait_ge(p.sem("mdone"), M_BASE[hq] + nm + 1)
                                    nm += 1
                                    psrc = ptm
                                else:
                                    te.wait_ge(p.sem("expdone"), S_BASE[hq] + j + 1)
                                    psrc = pt
                                te.matmul(ps[4][:], vh[:, 0, kt, :], psrc[:, j % 2],
                                          start=(j == 0), stop=(j == len(kts) - 1))
                                ins = te.matmul(ps[5][0:1, :], ones_bf[:, 0:1],
                                                psrc[:, j % 2],
                                                start=(j == 0), stop=(j == len(kts) - 1))
                                p.inc(ins, "avmm", 1)
                            te.wait_ge(p.sem("hilo"), hq + 1)
                            te.matmul(ps[6][:], ones2[0:1, :], nrmb1[0:1, hq % 2],
                                      start=True, stop=False)
                            p.inc(te.matmul(ps[6][:], ones2[0:1, :], nrmb2[0:1, hq % 2],
                                            start=False, stop=True), "bcmm", 1)

                @block.scalar
                def _(sc):
                    for h in range(NH):
                        for st in range(4):
                            sc.wait_ge(p.sem("kmm"), 4 * h + st + 1)
                            p.inc(sc.activation(kh[:, h % 2, st * 512:(st + 1) * 512],
                                                ps[2 + st % 2][:], AF.Identity,
                                                bias=bq[:, 16 + h:17 + h]), "kev", 1)
                        for qt in range(2):
                            sc.wait_ge(p.sem("qmm"), 2 * h + qt + 1)
                            p.inc(sc.activation(qh[:, h % 2, qt * 512:(qt + 1) * 512],
                                                ps[2 + qt][:], AF.Identity,
                                                bias=bq[:, h:h + 1]), "qev", 1)
                        for qt in range(2):
                            hq = h * 2 + qt
                            kts = ATT_KTS[qt]
                            for j, kt in enumerate(kts):
                                g = S_BASE[hq] + j
                                sc.wait_ge(p.sem("smm"), g + 1)
                                if g >= 2:
                                    sc.wait_ge(p.sem("avmm"), g - 1)
                                p.inc(sc.activation(pt[:, j % 2], ps[j % 2][:], AF.Exp,
                                                    scale=1.0 / 128.0,
                                                    bias=kb[:, h * 16 + kt:h * 16 + kt + 1]),
                                      "expdone", 1)
                            sc.wait_ge(p.sem("avmm"), S_BASE[hq + 1])
                            sc.wait_ge(p.sem("hilo"), hq)
                            p.inc(sc.activation(nrm[0:1, 0, :], ps[5][0:1, :], AF.Identity),
                                  "sev", 1)
                            sc.wait_ge(p.sem("avdone"), hq)
                            p.inc(sc.activation(avs[:], ps[4][:], AF.Identity), "avev", 1)
                            if hq >= 1:
                                sc.wait_ge(p.sem("avdone"), hq)
                                ph, pq = (hq - 1) // 2, (hq - 1) % 2
                                p.inc(sc.dma_start(
                                    out=adv[:, ph, pq * 512:(pq + 1) * 512],
                                    in_=ao[:, (hq - 1) % 2]), f"ast{(hq - 1) % 2}", 16)
                    sc.wait_ge(p.sem("avdone"), 32)
                    p.inc(sc.dma_start(out=adv[:, 15, 512:1024], in_=ao[:, 1]), "ast1", 16)

                @block.vector
                def _(ve):
                    for h in range(NH):
                        for qt in range(2):
                            hq = h * 2 + qt
                            kts = ATT_KTS[qt]
                            nm = 0
                            for j, kt in enumerate(kts):
                                if (qt, kt) in MASK_IDX:
                                    g = S_BASE[hq] + j
                                    ve.wait_ge(p.sem("expdone"), g + 1)
                                    if M_BASE[hq] + nm >= 2:
                                        ve.wait_ge(p.sem("avmm"), g - 1)
                                    p.inc(ve.tensor_tensor(ptm[:, j % 2], pt[:, j % 2],
                                                           masks[:, MASK_IDX[(qt, kt)], :],
                                                           ALU.min), "mdone", 1)
                                    nm += 1
                            ve.wait_ge(p.sem("sev"), hq + 1)
                            p.chain(ve, ve.reciprocal(nrm[0:1, 1, :], nrm[0:1, 0, :]), "mch")
                            p.chain(ve, ve.tensor_copy(nrmb1[0:1, hq % 2, :], nrm[0:1, 1, :]),
                                    "mch")
                            p.chain(ve, ve.tensor_copy(nrm[0:1, 2, :], nrmb1[0:1, hq % 2, :]),
                                    "mch")
                            p.chain(ve, ve.tensor_sub(nrm[0:1, 3, :], nrm[0:1, 1, :],
                                                      nrm[0:1, 2, :]), "mch")
                            p.inc(ve.tensor_copy(nrmb2[0:1, hq % 2, :], nrm[0:1, 3, :]),
                                  "hilo", 1)
                            ve.wait_ge(p.sem("avev"), hq + 1)
                            ve.wait_ge(p.sem("bcmm"), hq + 1)
                            if hq >= 2:
                                ve.wait_ge(p.sem(f"ast{hq % 2}"), 16 * (hq // 2))
                            p.inc(ve.tensor_tensor(ao[:, hq % 2], avs[:], ps[6][:],
                                                   ALU.mult), "avdone", 1)

        with (nc.sbuf_tensor("ats", [128, CD, T], BF16) as ats,
              nc.sbuf_tensor("wps", [128, 2, CD, 128], BF16) as wps,
              nc.sbuf_tensor("xt", [128, 2, 512], F32) as xt,
              nc.sbuf_tensor("prt", [128, 2, 512], F32) as prt,
              nc.sbuf_tensor("obp", [128, 2, 512], F32) as obp):
            with nc.Block() as block:
                @block.sync
                def _(sync):
                    sync.wait_ge(p.sem("ast0"), 16 * 16)
                    sync.wait_ge(p.sem("ast1"), 16 * 16)
                    p.inc(sync.dma_start(out=ats[:], in_=adv[:]), "lat", 16)
                    for i in range(32):
                        dout, qt = i // 2, i % 2
                        if i >= 2:
                            sync.wait_ge(p.sem("prdone"), i - 1)
                        p.inc(sync.dma_start(out=xt[:, i % 2],
                                             in_=xv[:, dout, qt * 512:(qt + 1) * 512]),
                              f"lxt{i % 2}", 16)

                @block.gpsimd
                def _(gp):
                    gp.wait_ge(p.sem("ast0"), 16 * 16)
                    gp.wait_ge(p.sem("ast1"), 16 * 16)
                    for dout in range(CD):
                        if dout >= 2:
                            gp.wait_ge(p.sem("pjmm"), 2 * (dout - 1))
                        p.inc(gp.dma_start(out=wps[:, dout % 2],
                                           in_=wpv[:, :, dout * 128:(dout + 1) * 128]),
                              f"lwp{dout % 2}", 16)

                @block.tensor
                def _(te):
                    te.wait_ge(p.sem("lat"), 16)
                    for i in range(32):
                        dout, qt = i // 2, i % 2
                        te.wait_ge(p.sem(f"lwp{dout % 2}"), 16 * (dout // 2 + 1))
                        if i >= 2:
                            te.wait_ge(p.sem("prv"), i - 1)
                        for c in range(CD):
                            ins = te.matmul(ps[i % 2][:], wps[:, dout % 2, c, :],
                                            ats[:, c, qt * 512:(qt + 1) * 512],
                                            start=(c == 0), stop=(c == CD - 1))
                        p.inc(ins, "pjmm", 1)

                @block.scalar
                def _(sc):
                    for i in range(32):
                        dout = i // 2
                        sc.wait_ge(p.sem("pjmm"), i + 1)
                        if i >= 2:
                            sc.wait_ge(p.sem("prdone"), i - 1)
                        p.inc(sc.activation(prt[:, i % 2], ps[i % 2][:], AF.Identity,
                                            bias=bp[:, dout:dout + 1]), "prv", 1)
                        if i >= 1:
                            sc.wait_ge(p.sem("prdone"), i)
                            pd, pq = (i - 1) // 2, (i - 1) % 2
                            p.inc(sc.dma_start(out=xov[:, pd, pq * 512:(pq + 1) * 512],
                                               in_=obp[:, (i - 1) % 2]),
                                  f"obst{(i - 1) % 2}", 16)
                    sc.wait_ge(p.sem("prdone"), 32)
                    p.inc(sc.dma_start(out=xov[:, 15, 512:1024], in_=obp[:, 1]), "obst1", 16)
                    sc.wait_ge(p.sem("obst0"), 16 * 16)
                    sc.wait_ge(p.sem("obst1"), 16 * 16)

                @block.vector
                def _(ve):
                    for i in range(32):
                        ve.wait_ge(p.sem("prv"), i + 1)
                        ve.wait_ge(p.sem(f"lxt{i % 2}"), 16 * (i // 2 + 1))
                        if i >= 2:
                            ve.wait_ge(p.sem(f"obst{i % 2}"), 16 * (i // 2))
                        p.inc(ve.tensor_add(obp[:, i % 2], prt[:, i % 2], xt[:, i % 2]),
                              "prdone", 1)

    return p.nc


def _attn_inputs(inp, x_full, kb_pc, qb_pc, masks_pc):
    wqkv = np.ascontiguousarray(np.asarray(inp["L0_attn_w_qkv"], np.float32))
    bq = np.asarray(inp["L0_attn_b_qkv"], np.float32)
    wproj = np.ascontiguousarray(np.asarray(inp["L0_attn_w_proj"], np.float32))
    bp = np.asarray(inp["L0_attn_b_proj"], np.float32)
    ones_bf = np.ones((128, 1), BF)
    ones2 = np.ones((2, 128), BF)
    cstf = np.zeros((1, 8), np.float32)
    cstf[0, 0] = LN_EPS
    cstf[0, 1] = LN_EPS_IN
    maps = []
    for c in range(NC):
        even = c - (c % 2)
        maps.append({
            "x": _fm_shard(x_full, c), "xfh": _fm_shard(x_full, even),
            "wqkv": wqkv, "bqkv_cm": _cm(bq), "bqkv_row": bq[None, :].astype(np.float32),
            "wproj": wproj, "bproj_cm": _cm(bp),
            "kbias": kb_pc[c], "qbias": qb_pc[c], "masks": masks_pc[c],
            "ones_bf": ones_bf, "ones2": ones2, "cstf": cstf,
        })
    return maps



FE = 4096
FECH = FE // 128


def build_l1ffn():
    p = P()
    nc = p.nc
    x = nc.dram_tensor("x", [D, T], F32, kind="ExternalInput").ap()
    wg = nc.dram_tensor("wgT", [D, 2], F32, kind="ExternalInput").ap()
    wa = [nc.dram_tensor(f"w{e}aT", [D, FE], BF16, kind="ExternalInput").ap() for e in range(2)]
    wb = [nc.dram_tensor(f"w{e}bT", [FE, D], BF16, kind="ExternalInput").ap() for e in range(2)]
    ba_cm = [nc.dram_tensor(f"b{e}a_cm", [128, FECH], F32, kind="ExternalInput").ap() for e in range(2)]
    bb_cm = [nc.dram_tensor(f"b{e}b_cm", [128, CD], F32, kind="ExternalInput").ap() for e in range(2)]
    ones_in = nc.dram_tensor("ones_bf", [128, 1], BF16, kind="ExternalInput").ap()
    cst_in = nc.dram_tensor("cstf", [1, 8], F32, kind="ExternalInput").ap()
    xo = nc.dram_tensor("xo", [D, T], F32, kind="ExternalOutput").ap()
    b1 = nc.dram_tensor("b1", [1, 512], F32).ap()
    b2 = nc.dram_tensor("b2", [1, 512], F32).ap()
    mb1 = nc.dram_tensor("mb1", [1, 512], F32).ap()
    mb2 = nc.dram_tensor("mb2", [1, 512], F32).ap()

    xv = chunked(x)
    wgv = chunked(wg)
    wav = [chunked(w) for w in wa]
    wbv = [chunked(w) for w in wb]
    xov = chunked(xo)
    ACTS = [AF.Gelu, AF.Relu]

    with p.stack:
        ones_bf = p.sbuf("ones_sb", [128, 1], BF16)
        cst = p.sbuf("cst", [1, 8], F32)
        wgs = p.sbuf("wgs", [128, CD, 2], BF16)
        bas = [p.sbuf(f"bas{e}", [128, FECH], F32) for e in range(2)]
        bbs = [p.sbuf(f"bbs{e}", [128, CD], F32) for e in range(2)]
        xs = p.sbuf("xs", [128, 2, CD, 512], F32)
        yb = p.sbuf("yb", [128, CD, 512], BF16)
        tmp = p.sbuf("tmp", [128, 2, 512], F32)
        xbf = p.sbuf("xbf", [128, 2, 512], BF16)
        xsq = p.sbuf("xsq", [128, 2, 512], BF16)
        mb = p.sbuf("mbx", [128, 512], F32)
        rb = p.sbuf("rbx", [128, 512], F32)
        rows = p.sbuf("rows", [1, 6, 512], F32)
        grow = p.sbuf("grow", [1, 5, 512], F32)
        m0b = p.sbuf("m0b", [128, 512], F32)
        m1b = p.sbuf("m1b", [128, 512], F32)
        was = p.sbuf("was", [128, 2, CD, 512], BF16)
        wbs = p.sbuf("wbs", [128, 2, 1024], BF16)
        hb = p.sbuf("hb", [128, FECH, 512], BF16)
        prt = p.sbuf("prt", [128, 2, 512], F32)
        act = p.sbuf("act", [128, 2, 512], F32)
        ob = p.sbuf("ob", [128, 2, 512], F32)
        srow, sqrow, r1, r2, sd, rstd = (rows[:, i] for i in range(6))
        ps = [p.psum(f"ps{i}", [128, 512]) for i in range(8)]

        with nc.Block() as block:
            @block.sync
            def _(sync):
                for dst, srcv in ((ones_bf[:], ones_in[:]), (cst[0:1, :], cst_in[0:1, :]),
                                  (bas[0][:], ba_cm[0][:]), (bas[1][:], ba_cm[1][:]),
                                  (bbs[0][:], bb_cm[0][:]), (bbs[1][:], bb_cm[1][:])):
                    p.inc(sync.dma_start(out=dst, in_=srcv), "lcst", 16)

            @block.gpsimd
            def _(gp):
                p.inc(gp.dma_start(out=wgs[:], in_=wgv[:]), "lwg", 16)

        for tt in range(2):
            tsl = slice(tt * 512, (tt + 1) * 512)
            with nc.Block() as block:
                @block.sync
                def _(sync, tt=tt):
                    if tt > 0:
                        sync.wait_ge(p.sem("obst0"), 16 * 8 * tt)
                        sync.wait_ge(p.sem("obst1"), 16 * 8 * tt)
                    p.inc(sync.dma_start(out=xs[:, 0], in_=xv[:, :, tsl]), f"lx{tt}", 16)
                emit_ln(p, block, f"ln{tt}_", xs[:, 0], yb, 512, ones_bf, b1, b2,
                        mb, rb, tmp, None, xbf, xsq, ps[4][0:1, :], ps[5][0:1, :],
                        cst[0:1, 0:1], srow, sqrow, r1, r2, sd, rstd,
                        gp_wait=[(f"lx{tt}", 16)],
                        sc_wait=[(f"lx{tt}", 16), ("lcst", 16 * 6)],
                        te_wait=[("lcst", 16 * 6), ("lwg", 16), ("prv", 32 * tt)],
                        prev=(f"ln{tt - 1}_" if tt > 0 else None))

            with nc.Block() as block:
                @block.tensor
                def _(te, tt=tt):
                    for e in range(2):
                        for c in range(CD):
                            ins = te.matmul(ps[6 + e][0:1, :], wgs[:, c, e:e + 1],
                                            yb[:, c], start=(c == 0), stop=(c == CD - 1))
                        p.inc(ins, "gmm", 1)

                @block.scalar
                def _(sc, tt=tt):
                    sc.wait_ge(p.sem("gmm"), 2 * (tt + 1))
                    if tt > 0:
                        sc.wait_ge(p.sem("mbld"), 32 * tt)
                    p.inc(sc.activation(grow[0:1, 0, :], ps[6][0:1, :], AF.Identity),
                          "gev", 1)
                    p.inc(sc.activation(grow[0:1, 1, :], ps[7][0:1, :], AF.Identity),
                          "gev", 1)
                    sc.wait_ge(p.sem("gmask"), tt + 1)
                    p.inc(sc.dma_start(out=mb1[0:1, :], in_=grow[0:1, 3, :]), "mbst", 16)
                    p.inc(sc.dma_start(out=mb2[0:1, :], in_=grow[0:1, 4, :]), "mbst", 16)
                    sc.wait_ge(p.sem("mbst"), 32 * (tt + 1))
                    p.inc(sc.dma_start(out=m0b[:], in_=mb1.broadcast_to((128, 512))), "mbld", 16)
                    p.inc(sc.dma_start(out=m1b[:], in_=mb2.broadcast_to((128, 512))), "mbld", 16)

                @block.vector
                def _(ve, tt=tt):
                    ve.wait_ge(p.sem("gev"), 2 * (tt + 1))
                    p.chain(ve, ve.tensor_tensor(grow[0:1, 2, :], grow[0:1, 0, :],
                                                 grow[0:1, 1, :], ALU.is_ge), "vch")
                    p.chain(ve, ve.tensor_scalar_mul(grow[0:1, 3, :], grow[0:1, 2, :], 0.5),
                            "vch")
                    p.inc(ve.tensor_scalar(grow[0:1, 4, :], grow[0:1, 2, :], -0.5, 0.5,
                                           ALU.mult, ALU.add), "gmask", 1)

            for e in range(2):
                with nc.Block() as block:
                    @block.sync
                    def _(sync, tt=tt, e=e):
                        for g in range(8):
                            G = (tt * 2 + e) * 8 + g
                            if G >= 2:
                                sync.wait_ge(p.sem("hmm"), 4 * (G - 1))
                            msl = slice(g * 512, (g + 1) * 512)
                            p.inc(sync.dma_start(out=was[:, G % 2], in_=wav[e][:, :, msl]),
                                  f"lwa{G % 2}", 16)
                        for dg in range(2):
                            for f in range(FECH):
                                FD = ((tt * 2 + e) * 2 + dg) * FECH + f
                                if FD >= 2:
                                    sync.wait_ge(p.sem("bmm"), FD - 1)
                                dsl = slice(dg * 1024, (dg + 1) * 1024)
                                p.inc(sync.dma_start(out=wbs[:, FD % 2], in_=wbv[e][:, f, dsl]),
                                      f"lwb{FD % 2}", 16)

                    @block.tensor
                    def _(te, tt=tt, e=e):
                        for g in range(8):
                            G = (tt * 2 + e) * 8 + g
                            te.wait_ge(p.sem(f"lwa{G % 2}"), 16 * (G // 2 + 1))
                            if g == 0:
                                te.wait_ge(p.sem(f"ln{tt}_y"), CD)
                                te.wait_ge(p.sem("prv"), 16 * (tt * 2 + e))
                            for m in range(4):
                                MH = G * 4 + m
                                if MH >= 4:
                                    te.wait_ge(p.sem("hev"), MH - 3)
                                pb = ps[MH % 4]
                                msl = slice(m * 128, (m + 1) * 128)
                                for c in range(CD):
                                    ins = te.matmul(pb[:], was[:, G % 2, c, msl], yb[:, c],
                                                    start=(c == 0), stop=(c == CD - 1))
                                p.inc(ins, "hmm", 1)
                        for dg in range(2):
                            DG = (tt * 2 + e) * 2 + dg
                            te.wait_ge(p.sem("hev"), 32 * (tt * 2 + e + 1))
                            te.wait_ge(p.sem("prv"), 8 * DG)
                            for f in range(FECH):
                                FD = DG * FECH + f
                                te.wait_ge(p.sem(f"lwb{FD % 2}"), 16 * (FD // 2 + 1))
                                for dout in range(8):
                                    ins = te.matmul(ps[dout][:],
                                                    wbs[:, FD % 2, dout * 128:(dout + 1) * 128],
                                                    hb[:, f], start=(f == 0),
                                                    stop=(f == FECH - 1))
                                p.inc(ins, "bmm", 1)

                    @block.scalar
                    def _(sc, tt=tt, e=e):
                        for g in range(8):
                            for m in range(4):
                                MH = ((tt * 2 + e) * 8 + g) * 4 + m
                                sc.wait_ge(p.sem("hmm"), MH + 1)
                                col = g * 4 + m
                                p.inc(sc.activation(hb[:, col], ps[MH % 4][:], ACTS[e],
                                                    bias=bas[e][:, col:col + 1]), "hev", 1)
                        for dg in range(2):
                            DG = (tt * 2 + e) * 2 + dg
                            for dout in range(8):
                                il = dg * 8 + dout
                                i = DG * 8 + dout
                                gi = 16 * (tt * 2 + e) + il
                                sc.wait_ge(p.sem("bmm"), (DG + 1) * FECH)
                                if gi >= 2:
                                    sc.wait_ge(p.sem("mdn"), gi - 1)
                                ch = dg * 8 + dout
                                p.inc(sc.activation(prt[:, i % 2], ps[dout][:], AF.Identity,
                                                    bias=bbs[e][:, ch:ch + 1]), "prv", 1)
                                if e == 1 and il >= 1:
                                    sc.wait_ge(p.sem("acc"), 32 * tt + 16 + il)
                                    pch = il - 1
                                    p.inc(sc.dma_start(out=xov[:, pch, tsl],
                                                       in_=ob[:, (il - 1) % 2]),
                                          f"obst{(il - 1) % 2}", 16)
                        if e == 1:
                            sc.wait_ge(p.sem("acc"), 32 * tt + 32)
                            p.inc(sc.dma_start(out=xov[:, 15, tsl], in_=ob[:, 1]),
                                  "obst1", 16)

                    @block.vector
                    def _(ve, tt=tt, e=e):
                        if e == 1:
                            ve.wait_ge(p.sem("acc"), 32 * tt + 16)
                        for dg in range(2):
                            for dout in range(8):
                                il = dg * 8 + dout
                                i = ((tt * 2 + e) * 2 + dg) * 8 + dout
                                gi = 16 * (tt * 2 + e) + il
                                ch = dg * 8 + dout
                                ve.wait_ge(p.sem("prv"), 16 * (tt * 2 + e) + il + 1)
                                if e == 0 and il == 0:
                                    ve.wait_ge(p.sem("mbld"), 32 * (tt + 1))
                                mk = m0b if e == 0 else m1b
                                p.chain(ve, ve.tensor_mul(act[:, il % 2], prt[:, il % 2],
                                                          mk[:]), "mdn")
                                if e == 0:
                                    ins = ve.tensor_add(xs[:, 1, ch, :], xs[:, 0, ch, :],
                                                        act[:, il % 2])
                                    p.inc(ins, "acc", 1)
                                else:
                                    if 8 * tt + il // 2 >= 1:
                                        ve.wait_ge(p.sem(f"obst{il % 2}"),
                                                   16 * (8 * tt + il // 2))
                                    ins = ve.tensor_add(ob[:, il % 2], xs[:, 1, ch, :],
                                                        act[:, il % 2])
                                    p.inc(ins, "acc", 1)



        with nc.Block() as block:
            @block.scalar
            def _(sc):
                sc.wait_ge(p.sem("obst0"), 16 * 16)
                sc.wait_ge(p.sem("obst1"), 16 * 16)

    return p.nc



def build_l1at1():
    p = P()
    nc = p.nc
    x = nc.dram_tensor("x", [D, T], F32, kind="ExternalInput").ap()
    wv_in = nc.dram_tensor("Wv", [D, D], F32, kind="ExternalInput").ap()
    g1 = nc.dram_tensor("ln1g_cm", [128, CD], F32, kind="ExternalInput").ap()
    bt1 = nc.dram_tensor("ln1b_cm", [128, CD], F32, kind="ExternalInput").ap()
    gt = nc.dram_tensor("tmg_cm", [128, CD], F32, kind="ExternalInput").ap()
    btt = nc.dram_tensor("tmb_cm", [128, CD], F32, kind="ExternalInput").ap()
    td = nc.dram_tensor("td_cm", [128, CD], F32, kind="ExternalInput").ap()
    rev = nc.dram_tensor("rev_t", [1, T], F32, kind="ExternalInput").ap()
    ones_in = nc.dram_tensor("ones_bf", [128, 1], BF16, kind="ExternalInput").ap()
    cst_in = nc.dram_tensor("cstf", [1, 8], F32, kind="ExternalInput").ap()
    yo = nc.dram_tensor("yo", [D, T], F32, kind="ExternalOutput").ap()
    xlno = nc.dram_tensor("xlno", [D, T], BF16, kind="ExternalOutput").ap()
    vmpo = nc.dram_tensor("vmpo", [128, CD], F32, kind="ExternalOutput").ap()
    b1 = nc.dram_tensor("b1", [1, 512], F32).ap()
    b2 = nc.dram_tensor("b2", [1, 512], F32).ap()
    rvb = nc.dram_tensor("rvb", [1, T], F32).ap()

    xv, yv, xlv = chunked(x), chunked(yo), chunked(xlno)
    wvv = chunked(wv_in)

    with p.stack:
        ones_bf = p.sbuf("ones_sb", [128, 1], BF16)
        cst = p.sbuf("cst", [1, 8], F32)
        g1s = p.sbuf("g1s", [128, CD], F32)
        b1s = p.sbuf("b1s", [128, CD], F32)
        gts = p.sbuf("gts", [128, CD], F32)
        bts = p.sbuf("bts", [128, CD], F32)
        tds = p.sbuf("tds", [128, CD], F32)
        revb = p.sbuf("revb", [128, T], F32)
        xs = p.sbuf("xs", [128, CD, 512], F32)
        yb = p.sbuf("ybf", [128, CD, 512], F32)
        x1b = p.sbuf("x1b", [128, CD, 512], F32)
        xlnb = p.sbuf("xlnb", [128, CD, T], BF16)
        tmp = p.sbuf("tmp", [128, 2, 512], F32)
        tmp2 = p.sbuf("tmp2", [128, 2, 512], F32)
        xbf = p.sbuf("xbf", [128, 2, 512], BF16)
        xsq = p.sbuf("xsq", [128, 2, 512], BF16)
        mb = p.sbuf("mbx", [128, 512], F32)
        rb = p.sbuf("rbx", [128, 512], F32)
        rows = p.sbuf("rows", [1, 6, 512], F32)
        wvs = p.sbuf("wvs", [128, 2, CD, 128], BF16)
        vt = p.sbuf("vt", [128, 2, 512], F32)
        dct = p.sbuf("dct", [128, 2, 512], F32)
        vdt = p.sbuf("vdt", [128, 2, 512], F32)
        vmpt = p.sbuf("vmpt", [128, 2, CD], F32)
        vmps = p.sbuf("vmps", [128, CD], F32)
        srow, sqrow, r1, r2, sd, rstd = (rows[:, i] for i in range(6))
        ps = [p.psum(f"ps{i}", [128, 512]) for i in range(8)]

        NCON = 9
        with nc.Block() as block:
            @block.sync
            def _(sync):
                for dst, srcv in ((ones_bf[:], ones_in[:]), (cst[0:1, :], cst_in[0:1, :]),
                                  (g1s[:], g1[:]), (b1s[:], bt1[:]), (gts[:], gt[:]),
                                  (bts[:], btt[:]), (tds[:], td[:])):
                    p.inc(sync.dma_start(out=dst, in_=srcv), "lcst", 16)
                p.inc(sync.dma_start(out=rvb[0:1, :], in_=rev[0:1, :]), "lcst", 16)
                sync.wait_ge(p.sem("lcst"), 16 * 8)
                p.inc(sync.dma_start(out=revb[:], in_=rvb.broadcast_to((128, T))), "lcst", 16)

        prev = None
        for tt in range(2):
            tsl = slice(tt * 512, (tt + 1) * 512)
            with nc.Block() as block:
                @block.sync
                def _(sync, tt=tt, tsl=tsl):
                    if tt > 0:
                        sync.wait_ge(p.sem(f"c{tt - 1}_y"), CD)
                    p.inc(sync.dma_start(out=xs[:], in_=xv[:, :, tsl]), f"lx{tt}", 16)
            for nm, src_, dst_, eps_i, gains in (
                    ("a", xs, yb, 0, None), ("b", yb, x1b, 1, (g1s, b1s)),
                    ("c", x1b, xlnb[:, :, tsl], 1, (gts, bts))):
                with nc.Block() as block:
                    gp_w = [(f"lx{tt}", 16)] if nm == "a" else [(f"{chr(ord(nm) - 1)}{tt}_y", CD)]
                    ve_w = []
                    if tt == 1 and nm == "a":
                        ve_w = [("yst0", 16)]
                    if tt == 1 and nm == "c":
                        ve_w = [("xlst0", 16)]
                    emit_ln(p, block, f"{nm}{tt}_", src_, dst_, 512, ones_bf, b1, b2,
                            mb, rb, tmp, tmp2, xbf, xsq, ps[4][0:1, :], ps[5][0:1, :],
                            cst[0:1, eps_i:eps_i + 1], srow, sqrow, r1, r2, sd, rstd,
                            gp_wait=gp_w, sc_wait=gp_w + [("lcst", 16 * 9)],
                            te_wait=[("lcst", 16 * 9)], ve_wait=ve_w,
                            gains=gains, prev=prev)
                prev = f"{nm}{tt}_"
            with nc.Block() as block:
                @block.scalar
                def _(sc, tt=tt, tsl=tsl):
                    sc.wait_ge(p.sem(f"a{tt}_y"), CD)
                    p.inc(sc.dma_start(out=yv[:, :, tsl], in_=yb[:]), f"yst{tt}", 16)
                    sc.wait_ge(p.sem(f"c{tt}_y"), CD)
                    p.inc(sc.dma_start(out=xlv[:, :, tsl], in_=xlnb[:, :, tsl]),
                          f"xlst{tt}", 16)

        with nc.Block() as block:
            @block.gpsimd
            def _(gp):
                for co in range(CD):
                    if co >= 2:
                        gp.wait_ge(p.sem("vmm"), 2 * (co - 1))
                    p.inc(gp.dma_start(out=wvs[:, co % 2],
                                       in_=wvv[:, :, co * 128:(co + 1) * 128]),
                          f"lwv{co % 2}", 16)

            @block.tensor
            def _(te):
                te.wait_ge(p.sem("c1_y"), CD)
                for co in range(CD):
                    te.wait_ge(p.sem(f"lwv{co % 2}"), 16 * (co // 2 + 1))
                    for tt in range(2):
                        i = co * 2 + tt
                        if i >= 2:
                            te.wait_ge(p.sem("vev"), i - 1)
                        for c in range(CD):
                            ins = te.matmul(ps[i % 2][:], wvs[:, co % 2, c, :],
                                            xlnb[:, c, tt * 512:(tt + 1) * 512],
                                            start=(c == 0), stop=(c == CD - 1))
                        p.inc(ins, "vmm", 1)

            @block.scalar
            def _(sc):
                for co in range(CD):
                    for tt in range(2):
                        i = co * 2 + tt
                        sc.wait_ge(p.sem("vmm"), i + 1)
                        if i >= 2:
                            sc.wait_ge(p.sem("vred"), i - 1)
                        p.inc(sc.activation(vt[:, i % 2], ps[i % 2][:], AF.Identity),
                              "vev", 1)
                        sc.wait_ge(p.sem("varg"), i + 1)
                        p.inc(sc.activation(dct[:, i % 2], vdt[:, i % 2], AF.Exp),
                              "vdec", 1)

            @block.vector
            def _(ve):
                ve.wait_ge(p.sem("lcst"), 16 * 9)
                for co in range(CD):
                    for tt in range(2):
                        i = co * 2 + tt
                        if i >= 2:
                            ve.wait_ge(p.sem("vdec"), i - 1)
                        p.inc(ve.tensor_scalar(vdt[:, i % 2],
                                               revb[:, tt * 512:(tt + 1) * 512],
                                               tds[:, co:co + 1], None, ALU.mult),
                              "varg", 1)
                        ve.wait_ge(p.sem("vev"), i + 1)
                        ve.wait_ge(p.sem("vdec"), i + 1)
                        p.chain(ve, ve.tensor_mul(vdt[:, i % 2], dct[:, i % 2], vt[:, i % 2])
                                if False else
                                ve.tensor_tensor(tmp[:, i % 2], dct[:, i % 2], vt[:, i % 2],
                                                 ALU.mult), "vch")
                        p.inc(ve.tensor_reduce(vmpt[:, tt, co:co + 1], tmp[:, i % 2],
                                               AX.X, ALU.add), "vred", 1)
                ve.wait_ge(p.sem("vred"), 32)
                p.inc(ve.tensor_add(vmps[:], vmpt[:, 0], vmpt[:, 1]), "vfin", 1)

            @block.scalar
            def _(sc):
                sc.wait_ge(p.sem("vfin"), 1)
                p.inc(sc.dma_start(out=vmpo[:], in_=vmps[:]), "vst", 16)
                sc.wait_ge(p.sem("vst"), 16)
                sc.wait_ge(p.sem("yst0"), 16)
                sc.wait_ge(p.sem("yst1"), 16)
                sc.wait_ge(p.sem("xlst0"), 16)
                sc.wait_ge(p.sem("xlst1"), 16)

    return p.nc



def build_l1at2():
    p = P()
    nc = p.nc
    y_in = nc.dram_tensor("y", [D, T], F32, kind="ExternalInput").ap()
    xln_in = nc.dram_tensor("xln", [D, T], BF16, kind="ExternalInput").ap()
    xlb = nc.dram_tensor("xlb_cm", [128, CD], BF16, kind="ExternalInput").ap()
    vmp_o = nc.dram_tensor("vmp_own", [128, CD], F32, kind="ExternalInput").ap()
    vmp_p = nc.dram_tensor("vmp_pair", [128, CD], F32, kind="ExternalInput").ap()
    tf = nc.dram_tensor("tf_cm", [128, CD], F32, kind="ExternalInput").ap()
    wr_in = nc.dram_tensor("Wr", [D, D], F32, kind="ExternalInput").ap()
    wo_in = nc.dram_tensor("Wo", [D, D], F32, kind="ExternalInput").ap()
    bo = nc.dram_tensor("bo_cm", [128, CD], F32, kind="ExternalInput").ap()
    g2 = nc.dram_tensor("ln2g_cm", [128, CD], F32, kind="ExternalInput").ap()
    b2g = nc.dram_tensor("ln2b_cm", [128, CD], F32, kind="ExternalInput").ap()
    gc = nc.dram_tensor("cmg_cm", [128, CD], F32, kind="ExternalInput").ap()
    bc = nc.dram_tensor("cmb_cm", [128, CD], F32, kind="ExternalInput").ap()
    ones_in = nc.dram_tensor("ones_bf", [128, 1], BF16, kind="ExternalInput").ap()
    cst_in = nc.dram_tensor("cstf", [1, 8], F32, kind="ExternalInput").ap()
    x2o = nc.dram_tensor("x2o", [D, T], F32, kind="ExternalOutput").ap()
    xco = nc.dram_tensor("xco", [D, T], BF16, kind="ExternalOutput").ap()
    b1 = nc.dram_tensor("b1", [1, 512], F32).ap()
    b2 = nc.dram_tensor("b2", [1, 512], F32).ap()

    yv, xlnv = chunked(y_in), chunked(xln_in)
    wrv, wov = chunked(wr_in), chunked(wo_in)
    x2v, xcv = chunked(x2o), chunked(xco)

    with p.stack:
        ones_bf = p.sbuf("ones_sb", [128, 1], BF16)
        cst = p.sbuf("cst", [1, 8], F32)
        tfs = p.sbuf("tfs", [128, CD], F32)
        tfm1 = p.sbuf("tfm1", [128, CD], F32)
        xlbs = p.sbuf("xlbs", [128, CD], BF16)
        bos = p.sbuf("bos", [128, CD], F32)
        g2s = p.sbuf("g2s", [128, CD], F32)
        b2s = p.sbuf("b2s", [128, CD], F32)
        gcs = p.sbuf("gcs", [128, CD], F32)
        bcs = p.sbuf("bcs", [128, CD], F32)
        vo = p.sbuf("vo", [128, CD], F32)
        vp = p.sbuf("vp", [128, CD], F32)
        vmx = p.sbuf("vmx", [128, CD], F32)
        vhl = p.sbuf("vhl", [128, CD, 2], BF16)
        vh32 = p.sbuf("vh32", [128, CD], F32)
        vlo = p.sbuf("vlo", [128, CD], F32)
        rbuf = p.sbuf("rbuf", [128, CD, T], BF16)
        zb = p.sbuf("zb", [128, CD], F32)
        zbb = p.sbuf("zbb", [128, CD], F32)
        zt = p.sbuf("zt", [128, 2, 2], F32)
        ws = p.sbuf("ws", [128, 2, CD, 128], BF16)
        tmp = p.sbuf("tmp", [128, 2, 512], F32)
        tmp2 = p.sbuf("tmp2", [128, 2, 512], F32)
        xbf = p.sbuf("xbf", [128, 2, 512], BF16)
        xsq = p.sbuf("xsq", [128, 2, 512], BF16)
        mb = p.sbuf("mbx", [128, 512], F32)
        rb = p.sbuf("rbx", [128, 512], F32)
        rows = p.sbuf("rows", [1, 6, 512], F32)
        srow, sqrow, r1, r2, sd, rstd = (rows[:, i] for i in range(6))
        ps = [p.psum(f"ps{i}", [128, 512]) for i in range(8)]
        NCON = 11

        with nc.Block() as block:
            @block.sync
            def _(sync):
                for dst, srcv in ((ones_bf[:], ones_in[:]), (cst[0:1, :], cst_in[0:1, :]),
                                  (tfs[:], tf[:]), (xlbs[:], xlb[:]), (bos[:], bo[:]),
                                  (g2s[:], g2[:]), (b2s[:], b2g[:]), (gcs[:], gc[:]),
                                  (bcs[:], bc[:]), (vo[:], vmp_o[:]), (vp[:], vmp_p[:])):
                    p.inc(sync.dma_start(out=dst, in_=srcv), "lcst", 16)

        with (nc.sbuf_tensor("xlnb", [128, CD, T], BF16) as xlnb,
              nc.sbuf_tensor("xsb", [128, CD, T], BF16) as xsb,
              nc.sbuf_tensor("mixb", [128, CD, T], BF16) as mixb):
            with nc.Block() as block:
                @block.sync
                def _(sync):
                    p.inc(sync.dma_start(out=xlnb[:], in_=xlnv[:]), "lxln", 16)

                @block.vector
                def _(ve):
                    ve.wait_ge(p.sem("lcst"), 16 * NCON)
                    p.inc(ve.tensor_scalar(tfm1[:], tfs[:], -1.0, 1.0, ALU.mult, ALU.add),
                          "tfm", 1)
                    ve.wait_ge(p.sem("lxln"), 16)
                    ve.wait_ge(p.sem("tfm"), 1)
                    for c in range(CD):
                        p.chain(ve, ve.tensor_copy(xsb[:, c, 1:T], xlnb[:, c, 0:T - 1]), "vch")
                        p.chain(ve, ve.tensor_copy(xsb[:, c, 0:1], xlbs[:, c:c + 1]), "vch")
                        p.chain(ve, ve.tensor_scalar(tmp[:, 0, :] if False else tmp.rearrange("p a b -> p (a b)")[:, 0:T],
                                                     xlnb[:, c], tfs[:, c:c + 1], None,
                                                     ALU.mult), "vch")
                        ins = ve.scalar_tensor_tensor(mixb[:, c], xsb[:, c], tfm1[:, c:c + 1],
                                                      tmp.rearrange("p a b -> p (a b)")[:, 0:T],
                                                      ALU.mult, ALU.add)
                        p.inc(ins, "mixd", 1)

                @block.gpsimd
                def _(gp):
                    for co in range(CD):
                        if co >= 2:
                            gp.wait_ge(p.sem("rmm"), 2 * (co - 1))
                        p.inc(gp.dma_start(out=ws[:, co % 2],
                                           in_=wrv[:, :, co * 128:(co + 1) * 128]),
                              f"lwr{co % 2}", 16)

                @block.tensor
                def _(te):
                    te.wait_ge(p.sem("mixd"), CD)
                    for co in range(CD):
                        te.wait_ge(p.sem(f"lwr{co % 2}"), 16 * (co // 2 + 1))
                        for tt in range(2):
                            i = co * 2 + tt
                            if i >= 2:
                                te.wait_ge(p.sem("rev_"), i - 1)
                            for c in range(CD):
                                ins = te.matmul(ps[i % 2][:], ws[:, co % 2, c, :],
                                                mixb[:, c, tt * 512:(tt + 1) * 512],
                                                start=(c == 0), stop=(c == CD - 1))
                            p.inc(ins, "rmm", 1)

                @block.scalar
                def _(sc):
                    for co in range(CD):
                        for tt in range(2):
                            i = co * 2 + tt
                            sc.wait_ge(p.sem("rmm"), i + 1)
                            p.inc(sc.activation(rbuf[:, co, tt * 512:(tt + 1) * 512],
                                                ps[i % 2][:], AF.Sigmoid), "rev_", 1)

        with nc.Block() as block:
            @block.vector
            def _(ve):
                ve.wait_ge(p.sem("lcst"), 16 * NCON)
                p.chain(ve, ve.tensor_add(vmx[:], vo[:], vp[:]), "vch")
                p.chain(ve, ve.tensor_copy(vhl[:, :, 0], vmx[:]), "vch")
                p.chain(ve, ve.tensor_copy(vh32[:], vhl[:, :, 0]), "vch")
                p.chain(ve, ve.tensor_sub(vlo[:], vmx[:], vh32[:]), "vch")
                p.inc(ve.tensor_copy(vhl[:, :, 1], vlo[:]), "vhld", 1)

            @block.gpsimd
            def _(gp):
                gp.wait_ge(p.sem("rmm"), 32)
                for co in range(CD):
                    if co >= 2:
                        gp.wait_ge(p.sem("zmm"), co - 1)
                    p.inc(gp.dma_start(out=ws[:, co % 2],
                                       in_=wov[:, :, co * 128:(co + 1) * 128]),
                          f"lwo{co % 2}", 16)

            @block.tensor
            def _(te):
                te.wait_ge(p.sem("vhld"), 1)
                te.wait_ge(p.sem("rmm"), 32)
                for co in range(CD):
                    te.wait_ge(p.sem(f"lwo{co % 2}"), 16 * (co // 2 + 1))
                    if co >= 2:
                        te.wait_ge(p.sem("zev"), co - 1)
                    for c in range(CD):
                        ins = te.matmul(ps[co % 2][:, 0:2], ws[:, co % 2, c, :],
                                        vhl[:, c, :], start=(c == 0), stop=(c == CD - 1))
                    p.inc(ins, "zmm", 1)

            @block.scalar
            def _(sc):
                for co in range(CD):
                    sc.wait_ge(p.sem("zmm"), co + 1)
                    if co >= 2:
                        sc.wait_ge(p.sem("zbd"), co - 1)
                    p.inc(sc.activation(zt[:, co % 2, :], ps[co % 2][:, 0:2], AF.Identity),
                          "zev", 1)

            @block.vector
            def _(ve):
                for co in range(CD):
                    ve.wait_ge(p.sem("zev"), co + 1)
                    p.chain(ve, ve.tensor_add(zb[:, co:co + 1], zt[:, co % 2, 0:1],
                                              zt[:, co % 2, 1:2]), "vch")
                    p.inc(ve.tensor_scalar(zbb[:, co:co + 1], zb[:, co:co + 1],
                                           bos[:, co:co + 1], None, ALU.add), "zbd", 1)

        with (nc.sbuf_tensor("ybA", [128, CD, 512], F32) as ybA,
              nc.sbuf_tensor("ybB", [128, CD, 512], F32) as ybB,
              nc.sbuf_tensor("xct", [128, CD, 512], BF16) as xct):
            prev = None
            for tt in range(2):
                tsl = slice(tt * 512, (tt + 1) * 512)
                with nc.Block() as block:
                    @block.sync
                    def _(sync, tt=tt, tsl=tsl):
                        sync.wait_ge(p.sem("zbd"), CD)
                        if tt > 0:
                            sync.wait_ge(p.sem(f"e{tt - 1}_y"), CD)
                        p.inc(sync.dma_start(out=ybA[:], in_=yv[:, :, tsl]), f"ly{tt}", 16)

                    @block.vector
                    def _(ve, tt=tt, tsl=tsl):
                        ve.wait_ge(p.sem(f"ly{tt}", ), 16)
                        ve.wait_ge(p.sem("zbd"), CD)
                        if tt > 0:
                            ve.wait_ge(p.sem(f"x2st{tt - 1}"), 16)
                        for c in range(CD):
                            p.chain(ve, ve.tensor_scalar(tmp[:, c % 2], rbuf[:, c, tsl],
                                                         zbb[:, c:c + 1], None, ALU.mult),
                                    "vch")
                            ins = ve.tensor_add(ybB[:, c], ybA[:, c], tmp[:, c % 2])
                            p.inc(ins, f"x2d{tt}", 1)

                    @block.scalar
                    def _(sc, tt=tt, tsl=tsl):
                        sc.wait_ge(p.sem(f"x2d{tt}"), CD)
                        p.inc(sc.dma_start(out=x2v[:, :, tsl], in_=ybB[:]), f"x2st{tt}", 16)

                for nm, src_, dst_, gains in (("d", ybB, ybA, (g2s, b2s)),
                                              ("e", ybA, xct, (gcs, bcs))):
                    with nc.Block() as block:
                        gpw = ([(f"x2d{tt}", CD)] if nm == "d" else [(f"d{tt}_y", CD)])
                        vew = ([(f"x2st{tt}", 16)] if nm == "d"
                               else ([("xcst0", 16)] if tt == 1 else []))
                        emit_ln(p, block, f"{nm}{tt}_", src_, dst_, 512, ones_bf, b1, b2,
                                mb, rb, tmp, tmp2, xbf, xsq, ps[4][0:1, :], ps[5][0:1, :],
                                cst[0:1, 1:2], srow, sqrow, r1, r2, sd, rstd,
                                gp_wait=gpw, sc_wait=gpw + [("lcst", 16 * NCON)],
                                te_wait=[("lcst", 16 * NCON), ("zmm", CD)], ve_wait=vew,
                                gains=gains, prev=prev)
                    prev = f"{nm}{tt}_"
                with nc.Block() as block:
                    @block.scalar
                    def _(sc, tt=tt, tsl=tsl):
                        sc.wait_ge(p.sem(f"e{tt}_y"), CD)
                        p.inc(sc.dma_start(out=xcv[:, :, tsl], in_=xct[:]), f"xcst{tt}", 16)

        with nc.Block() as block:
            @block.scalar
            def _(sc):
                for s in ("x2st0", "x2st1", "xcst0", "xcst1"):
                    sc.wait_ge(p.sem(s), 16)

    return p.nc



def build_l1at3():
    p = P()
    nc = p.nc
    x = nc.dram_tensor("x", [D, T], F32, kind="ExternalInput").ap()
    x2 = nc.dram_tensor("x2", [D, T], F32, kind="ExternalInput").ap()
    xc = nc.dram_tensor("xc", [D, T], BF16, kind="ExternalInput").ap()
    xcb = nc.dram_tensor("xcb_cm", [128, CD], BF16, kind="ExternalInput").ap()
    wck = nc.dram_tensor("Wck16", [D, 8192], BF16, kind="ExternalInput").ap()
    wcv = nc.dram_tensor("Wcv16", [8192, D], BF16, kind="ExternalInput").ap()
    wcr = nc.dram_tensor("Wcr", [D, D], F32, kind="ExternalInput").ap()
    ones_in = nc.dram_tensor("ones_bf", [128, 1], BF16, kind="ExternalInput").ap()
    cst_in = nc.dram_tensor("cstf", [1, 8], F32, kind="ExternalInput").ap()
    xo = nc.dram_tensor("xo", [D, T], F32, kind="ExternalOutput").ap()

    xv, x2v, xcv, xov = chunked(x), chunked(x2), chunked(xc), chunked(xo)
    wckv, wcvv, wcrv = chunked(wck), chunked(wcv), chunked(wcr)
    FK = 64

    with p.stack:
        ones_bf = p.sbuf("ones_sb", [128, 1], BF16)
        cst = p.sbuf("cst", [1, 8], F32)
        xcbs = p.sbuf("xcbs", [128, CD], BF16)
        xcnb = p.sbuf("xcnb", [128, CD, T], BF16)
        xcsb = p.sbuf("xcsb", [128, CD, T], BF16)
        rc = p.sbuf("rc", [128, CD, T], BF16)
        wks = p.sbuf("wks", [128, 2, CD, 128], BF16)
        wvs = p.sbuf("wvs", [128, 2, 1024], BF16)
        kcb = p.sbuf("kcb", [128, FK, 512], BF16)
        rtb = p.sbuf("rtb", [128, 2, 512], BF16)
        vcb = p.sbuf("vcb", [128, 2, 512], BF16)
        t1b = p.sbuf("t1b", [128, 2, 512], F32)
        o1b = p.sbuf("o1b", [128, 2, 512], F32)
        x2t = p.sbuf("x2t", [128, 2, 512], F32)
        xt = p.sbuf("xt", [128, 2, 512], F32)
        ob = p.sbuf("ob", [128, 2, 512], F32)
        ps = [p.psum(f"ps{i}", [128, 512]) for i in range(8)]

        with nc.Block() as block:
            @block.sync
            def _(sync):
                for dst, srcv in ((ones_bf[:], ones_in[:]), (cst[0:1, :], cst_in[0:1, :]),
                                  (xcbs[:], xcb[:])):
                    p.inc(sync.dma_start(out=dst, in_=srcv), "lcst", 16)
                p.inc(sync.dma_start(out=xcnb[:], in_=xcv[:]), "lxc", 16)

            @block.vector
            def _(ve):
                ve.wait_ge(p.sem("lcst"), 16 * 3)
                ve.wait_ge(p.sem("lxc"), 16)
                for c in range(CD):
                    p.chain(ve, ve.tensor_copy(xcsb[:, c, 1:T], xcnb[:, c, 0:T - 1]), "vch")
                    ins = ve.tensor_copy(xcsb[:, c, 0:1], xcbs[:, c:c + 1])
                    p.inc(ins, "xcsd", 1)

            @block.gpsimd
            def _(gp):
                for co in range(CD):
                    if co >= 2:
                        gp.wait_ge(p.sem("rcmm"), 2 * (co - 1))
                    p.inc(gp.dma_start(out=wks[:, co % 2],
                                       in_=wcrv[:, :, co * 128:(co + 1) * 128]),
                          f"lwcr{co % 2}", 16)

            @block.tensor
            def _(te):
                te.wait_ge(p.sem("lxc"), 16)
                for co in range(CD):
                    te.wait_ge(p.sem(f"lwcr{co % 2}"), 16 * (co // 2 + 1))
                    for tt in range(2):
                        i = co * 2 + tt
                        if i >= 2:
                            te.wait_ge(p.sem("rcev"), i - 1)
                        for c in range(CD):
                            ins = te.matmul(ps[i % 2][:], wks[:, co % 2, c, :],
                                            xcnb[:, c, tt * 512:(tt + 1) * 512],
                                            start=(c == 0), stop=(c == CD - 1))
                        p.inc(ins, "rcmm", 1)

            @block.scalar
            def _(sc):
                for co in range(CD):
                    for tt in range(2):
                        i = co * 2 + tt
                        sc.wait_ge(p.sem("rcmm"), i + 1)
                        p.inc(sc.activation(rc[:, co, tt * 512:(tt + 1) * 512],
                                            ps[i % 2][:], AF.Sigmoid), "rcev", 1)

        for tt in range(2):
            tsl = slice(tt * 512, (tt + 1) * 512)
            with nc.Block() as block:
                @block.gpsimd
                def _(gp, tt=tt):
                    for f in range(FK):
                        FG = tt * FK + f
                        if FG < 2:
                            gp.wait_ge(p.sem("rcmm"), 32)
                        else:
                            gp.wait_ge(p.sem("kmm"), FG - 1)
                        p.inc(gp.dma_start(out=wks[:, FG % 2],
                                           in_=wckv[:, :, f * 128:(f + 1) * 128]),
                              f"lwck{FG % 2}", 16)

                @block.tensor
                def _(te, tt=tt, tsl=tsl):
                    if tt == 0:
                        te.wait_ge(p.sem("rcmm"), 32)
                    te.wait_ge(p.sem("xcsd"), CD)
                    for f in range(FK):
                        FG = tt * FK + f
                        te.wait_ge(p.sem(f"lwck{FG % 2}"), 16 * (FG // 2 + 1))
                        if FG >= 2:
                            te.wait_ge(p.sem("ksq"), FG - 1)
                        for c in range(CD):
                            ins = te.matmul(ps[FG % 2][:], wks[:, FG % 2, c, :],
                                            xcsb[:, c, tsl], start=(c == 0),
                                            stop=(c == CD - 1))
                        p.inc(ins, "kmm", 1)

                @block.scalar
                def _(sc, tt=tt):
                    for f in range(FK):
                        FG = tt * FK + f
                        sc.wait_ge(p.sem("kmm"), FG + 1)
                        p.chain(sc, sc.activation(rtb[:, FG % 2], ps[FG % 2][:], AF.Relu),
                                "sch")
                        p.inc(sc.activation(kcb[:, f], rtb[:, FG % 2], AF.Square), "ksq", 1)

            with nc.Block() as block:
                @block.gpsimd
                def _(gp, tt=tt):
                    for dg in range(2):
                        for f in range(FK):
                            FD = (tt * 2 + dg) * FK + f
                            if FD >= 2:
                                gp.wait_ge(p.sem("vmm"), FD - 1)
                            dsl = slice(dg * 1024, (dg + 1) * 1024)
                            p.inc(gp.dma_start(out=wvs[:, FD % 2], in_=wcvv[:, f, dsl]),
                                  f"lwcv{FD % 2}", 16)

                @block.sync
                def _(sync, tt=tt, tsl=tsl):
                    for i in range(CD):
                        gi = tt * CD + i
                        if gi >= 2:
                            sync.wait_ge(p.sem("obd"), gi - 1)
                        p.inc(sync.dma_start(out=x2t[:, gi % 2], in_=x2v[:, i, tsl]),
                              f"lx2{gi % 2}", 16)
                        p.inc(sync.dma_start(out=xt[:, gi % 2], in_=xv[:, i, tsl]),
                              f"lxr{gi % 2}", 16)

                @block.tensor
                def _(te, tt=tt):
                    te.wait_ge(p.sem("ksq"), FK * (tt + 1))
                    for dg in range(2):
                        DG = tt * 2 + dg
                        te.wait_ge(p.sem("vcev"), 8 * DG)
                        for f in range(FK):
                            FD = DG * FK + f
                            te.wait_ge(p.sem(f"lwcv{FD % 2}"), 16 * (FD // 2 + 1))
                            for dout in range(8):
                                ins = te.matmul(ps[dout][:],
                                                wvs[:, FD % 2, dout * 128:(dout + 1) * 128],
                                                kcb[:, f], start=(f == 0), stop=(f == FK - 1))
                            p.inc(ins, "vmm", 1)

                @block.scalar
                def _(sc, tt=tt, tsl=tsl):
                    for dg in range(2):
                        DG = tt * 2 + dg
                        for dout in range(8):
                            gi = DG * 8 + dout
                            sc.wait_ge(p.sem("vmm"), (DG + 1) * FK)
                            if gi >= 2:
                                sc.wait_ge(p.sem("t1d"), gi - 1)
                            p.inc(sc.activation(vcb[:, gi % 2], ps[dout][:], AF.Identity),
                                  "vcev", 1)
                            if gi >= 1:
                                sc.wait_ge(p.sem("obd"), gi)
                                pch = (gi - 1) % CD
                                ptt = (gi - 1) // CD
                                psl = slice(ptt * 512, (ptt + 1) * 512)
                                p.inc(sc.dma_start(out=xov[:, pch, psl],
                                                   in_=ob[:, (gi - 1) % 2]),
                                      f"obst{(gi - 1) % 2}", 16)

                @block.vector
                def _(ve, tt=tt, tsl=tsl):
                    ve.wait_ge(p.sem("rcev"), 32)
                    for dg in range(2):
                        DG = tt * 2 + dg
                        for dout in range(8):
                            gi = DG * 8 + dout
                            ch = dg * 8 + dout
                            ve.wait_ge(p.sem("vcev"), gi + 1)
                            ve.wait_ge(p.sem(f"lx2{gi % 2}"), 16 * (gi // 2 + 1))
                            ve.wait_ge(p.sem(f"lxr{gi % 2}"), 16 * (gi // 2 + 1))
                            if gi >= 2:
                                ve.wait_ge(p.sem(f"obst{gi % 2}"), 16 * (gi // 2))
                            p.chain(ve, ve.tensor_tensor(t1b[:, gi % 2], vcb[:, gi % 2],
                                                         rc[:, ch, tsl], ALU.mult), "t1d")
                            p.chain(ve, ve.tensor_add(o1b[:, gi % 2], t1b[:, gi % 2],
                                                      x2t[:, gi % 2]), "vch")
                            p.inc(ve.tensor_add(ob[:, gi % 2], o1b[:, gi % 2],
                                                xt[:, gi % 2]), "obd", 1)

        with nc.Block() as block:
            @block.scalar
            def _(sc):
                sc.wait_ge(p.sem("obd"), 32)
                p.inc(sc.dma_start(out=xov[:, 15, 512:1024], in_=ob[:, 1]), "obst1", 16)
                sc.wait_ge(p.sem("obst0"), 16 * 16)
                sc.wait_ge(p.sem("obst1"), 16 * 16)

    return p.nc



_CST = None


def _cstf():
    c = np.zeros((1, 8), np.float32)
    c[0, 0] = LN_EPS
    c[0, 1] = LN_EPS_IN
    return c


def _gather_fm(results, key):
    return np.vstack([np.asarray(results[c][key], np.float32).T for c in range(NC)])


def kernel(**inp):
    ones_bf = np.ones((128, 1), BF)
    cst = _cstf()

    ncA = build_prep()
    resA = run_bass_kernel_spmd(ncA, _prep_inputs(inp), core_ids=CORES)
    W = _prep_gather(resA.results)

    x_full = np.ascontiguousarray(
        np.asarray(inp["L0_attn_hidden_states"], np.float32).reshape(NC * T, D))

    ncB = build_l0ffn()

    def run_l0ffn(xf):
        maps = [{"x": _fm_shard(xf, c), "q1T": W["q1T"], "q2T": W["q2T"],
                 "q3T": W["q3T"], "ones_bf": ones_bf, "cstf": cst} for c in range(NC)]
        res = run_bass_kernel_spmd(ncB, maps, core_ids=CORES)
        return _gather_fm(res.results, "xo")

    x1 = run_l0ffn(x_full)

    kb_pc, qb_pc, masks_pc = _attn_consts()
    ncC = build_l0attn()
    res = run_bass_kernel_spmd(ncC, _attn_inputs(inp, x1, kb_pc, qb_pc, masks_pc),
                               core_ids=CORES)
    x2 = _gather_fm(res.results, "xo")

    x3 = run_l0ffn(x2)

    ncE = build_l1ffn()
    wgT = np.ascontiguousarray(np.asarray(inp["L1_ffn_Wg"], np.float32).T)
    eb = {"wgT": wgT,
          "w0aT": np.ascontiguousarray(W["W0aT16"]), "w0bT": np.ascontiguousarray(W["W0bT16"]),
          "w1aT": np.ascontiguousarray(W["W1aT16"]), "w1bT": np.ascontiguousarray(W["W1bT16"]),
          "b0a_cm": _cm(inp["L1_ffn_b0a"]), "b0b_cm": _cm(inp["L1_ffn_b0b"]),
          "b1a_cm": _cm(inp["L1_ffn_b1a"]), "b1b_cm": _cm(inp["L1_ffn_b1b"]),
          "ones_bf": ones_bf, "cstf": cst}

    def run_l1ffn(xf):
        maps = [dict(eb, x=_fm_shard(xf, c)) for c in range(NC)]
        res = run_bass_kernel_spmd(ncE, maps, core_ids=CORES)
        return _gather_fm(res.results, "xo")

    x4 = run_l1ffn(x3)

    ncF1 = build_l1at1()
    maps = []
    for c in range(NC):
        rev = (2047.0 - (np.arange(T) + (c % 2) * 1024)).astype(np.float32)[None, :]
        maps.append({"x": _fm_shard(x4, c),
                     "Wv": np.ascontiguousarray(np.asarray(inp["L1_attn_Wv"], np.float32)),
                     "ln1g_cm": _cm(inp["L1_attn_ln1_g"]), "ln1b_cm": _cm(inp["L1_attn_ln1_b"]),
                     "tmg_cm": _cm(inp["L1_attn_tm_g"]), "tmb_cm": _cm(inp["L1_attn_tm_b"]),
                     "td_cm": _cm(inp["L1_attn_time_decay"]), "rev_t": rev,
                     "ones_bf": ones_bf, "cstf": cst})
    resF1 = run_bass_kernel_spmd(ncF1, maps, core_ids=CORES)
    r1 = resF1.results

    ncF2 = build_l1at2()
    maps = []
    for c in range(NC):
        even = c - (c % 2)
        if c % 2 == 0:
            xlb = np.zeros((128, CD), BF)
        else:
            xlb = np.ascontiguousarray(
                np.asarray(r1[even]["xlno"])[:, T - 1].reshape(CD, 128).T)
        maps.append({"y": np.asarray(r1[c]["yo"]), "xln": np.asarray(r1[c]["xlno"]),
                     "xlb_cm": xlb, "vmp_own": np.asarray(r1[c]["vmpo"]),
                     "vmp_pair": np.asarray(r1[c ^ 1]["vmpo"]),
                     "tf_cm": _cm(inp["L1_attn_time_first"]),
                     "Wr": np.ascontiguousarray(np.asarray(inp["L1_attn_Wr"], np.float32)),
                     "Wo": np.ascontiguousarray(np.asarray(inp["L1_attn_Wo"], np.float32)),
                     "bo_cm": _cm(inp["L1_attn_bo"]),
                     "ln2g_cm": _cm(inp["L1_attn_ln2_g"]), "ln2b_cm": _cm(inp["L1_attn_ln2_b"]),
                     "cmg_cm": _cm(inp["L1_attn_cm_g"]), "cmb_cm": _cm(inp["L1_attn_cm_b"]),
                     "ones_bf": ones_bf, "cstf": cst})
    resF2 = run_bass_kernel_spmd(ncF2, maps, core_ids=CORES)
    r2 = resF2.results

    ncF3 = build_l1at3()
    wck16 = np.ascontiguousarray(W["Wck16"])
    wcv16 = np.ascontiguousarray(W["Wcv16"])
    wcr = np.ascontiguousarray(np.asarray(inp["L1_attn_Wcr"], np.float32))
    maps = []
    for c in range(NC):
        even = c - (c % 2)
        if c % 2 == 0:
            xcb = np.zeros((128, CD), BF)
        else:
            xcb = np.ascontiguousarray(
                np.asarray(r2[even]["xco"])[:, T - 1].reshape(CD, 128).T)
        maps.append({"x": _fm_shard(x4, c), "x2": np.asarray(r2[c]["x2o"]),
                     "xc": np.asarray(r2[c]["xco"]), "xcb_cm": xcb,
                     "Wck16": wck16, "Wcv16": wcv16, "Wcr": wcr,
                     "ones_bf": ones_bf, "cstf": cst})
    resF3 = run_bass_kernel_spmd(ncF3, maps, core_ids=CORES)
    x5 = _gather_fm(resF3.results, "xo")

    x6 = run_l1ffn(x5)

    return np.ascontiguousarray(x6.reshape(4, 2048, 2048))
